# version 20
# baseline (speedup 1.0000x reference)
import numpy as np
import ml_dtypes
import concourse.bass as bass
import concourse.mybir as mybir
from concourse.bass_utils import run_bass_kernel_spmd
from contextlib import ExitStack

F32 = mybir.dt.float32
BF16 = mybir.dt.bfloat16
I32 = mybir.dt.int32
Act = mybir.ActivationFunctionType
Alu = mybir.AluOpType
AX = mybir.AxisListType

D, H, HD, C, IF = 1024, 8, 128, 128, 2816
NKC = D // 128
NIC = IF // 128
MAGIC = 12582912.0
VELT = float(2**22 + 1)
DBG = False
PANW = [512, 512, 512, 512, 512, 256]
PANO = [0, 512, 1024, 1536, 2048, 2560]

qU, qC1, qT1, qXQ = 0, 1024, 2048, 3072
wLD = [4096, 5120]
qM, qP, qR = 6144, 6400, 6656
mXT, mXN, mKT, mKBT, mOT = 7168, 8192, 9216, 10240, 11264
mKFM, mKBFM, mKDEF = 12288, 13312, 14336
mRR = 15360
mATTT, mKDW, mATT = 17408, 18432, 19456
mPP, mPPT = 20480, 21504
mKCDT, mVNEW = mPPT, mPP
mSPILL, mSPILLW = 14336, 5120
fHT = 4096
fXT, fXN = 6144, 7168
fST, fGH, fOT = 8192, 9216, 10240
fSQ3 = 16896
fXN2, fST2, fGH2, fOT2 = 17920, 18944, 19968, 20992
bOF2 = 22528

bTH, bLO, bM, bSEL, bQH, bQL = 0, 1024, 2048, 3072, 4096, 5120
bXQH, bXQL = 6144, 10240
bWP = [14336, 22528]
bXFH, bXFL = 6144, 7168
bWD_H, bWD_L = 0, 11264
bOFH, bOFL = 28160, 29184
fSTG = [[0, 1408], [2816, 4224]]
fFM = [[11264, 12672], [14080, 15488]]


def build_nc(t_own=2048, t_pre=2048, n_cores=8):
    nc = bass.Bass("TRN2", target_bir_lowering=False)
    NT = t_own // 128
    NCH = t_own // C
    NCHP = t_pre // C

    xo_d = nc.dram_tensor("xo", [t_own, D], F32, kind="ExternalInput").ap()
    sm_d = nc.dram_tensor("smask", [1, 1], F32, kind="ExternalInput").ap()
    Ww_d = nc.dram_tensor("Ww", [D, D], F32, kind="ExternalInput").ap()
    Wr_d = nc.dram_tensor("Wr", [D, D], F32, kind="ExternalInput").ap()
    Wb_d = nc.dram_tensor("Wb", [H, D], F32, kind="ExternalInput").ap()
    dec_d = nc.dram_tensor("dec", [1, H], F32, kind="ExternalInput").ap()
    wg_d = nc.dram_tensor("wg", [IF, D], F32, kind="ExternalInput").ap()
    wu_d = nc.dram_tensor("wu", [IF, D], F32, kind="ExternalInput").ap()
    wd_d = nc.dram_tensor("wd", [D, IF], F32, kind="ExternalInput").ap()
    y_d = nc.dram_tensor("y", [t_own, D], F32, kind="ExternalOutput").ap()
    swg = [nc.dram_tensor(f"swg{p}", [IF, D], BF16, kind="Internal").ap() for p in range(2)]
    swu = [nc.dram_tensor(f"swu{p}", [IF, D], BF16, kind="Internal").ap() for p in range(2)]
    swd = [nc.dram_tensor(f"swd{p}", [D, IF], BF16, kind="Internal").ap() for p in range(2)]
    swt = [nc.dram_tensor(f"swt{p}", [128, NKC * D], BF16, kind="Internal").ap() for p in range(2)]
    shq = [nc.dram_tensor(f"shq{p}", [t_own, IF], BF16, kind="Internal").ap() for p in range(2)]
    so_d = nc.dram_tensor("so", [t_own, D], F32, kind="Internal").ap()
    sck_d = nc.dram_tensor("sck", [t_own, 5120], F32, kind="Internal").ap()
    ccin_t = nc.dram_tensor("ccin", [128, H * 128], F32)
    ccout_t = nc.dram_tensor("ccout", [128, H * 128], F32)
    dbg = {}
    if DBG:
        for nm, wdt in [("xn", D), ("kt", D), ("R0", 2048), ("A0", 1024),
                        ("RN", 2048), ("att", 1024), ("vnew", 1024), ("S1", 1024),
                        ("S2", 1024), ("bet", H), ("kdw", D), ("kdef", D),
                        ("dwc", H), ("dec2", H), ("ctdc", H), ("Lmt", 1024),
                        ("nLst", 1024), ("mtit", 128)]:
            dbg[nm] = nc.dram_tensor("dbg_" + nm, [128, wdt], F32, kind="Internal").ap()
    sg_d = nc.dram_tensor("sgh", [t_own, 512], F32, kind="Internal").ap()

    with ExitStack() as S_:
        sb = lambda n, s, d: S_.enter_context(nc.sbuf_tensor(n, s, d))
        idn = sb("idn", [128, 128], F32)
        idnb = sb("idnb", [128, 128], BF16)
        iot = sb("iot", [128, 128], I32)
        irow = sb("irow", [128, 128], I32)
        idf = sb("idf", [128, 128], F32)
        irf = sb("irf", [128, 128], F32)
        mti = sb("mti", [128, 128], F32)
        mts = sb("mts", [128, 128], F32)
        lgb = sb("lgb", [128, H], F32)
        smb = sb("smb", [128, 1], F32)
        ismb = sb("ismb", [128, 1], F32)
        de_c = sb("de_c", [128, H], F32)
        dw_c = sb("dw_c", [128, H], F32)
        ctd_c = sb("ctd_c", [128, H], F32)
        de_f = sb("de_f", [128, H * 128], F32)
        Lm = sb("Lm", [128, H * 128], F32)
        nLs = sb("nLs", [128, H * 128], F32)
        Smat = sb("Smat", [128, H * 128], F32)
        colv = sb("colv", [128, 1], F32)
        rn = sb("rn", [128, H], F32)
        bet = sb("bet", [128, H], F32)
        lnm = sb("lnm", [128, 1], F32)
        lnv = sb("lnv", [128, 1], F32)
        WT_h = sb("WT_h", [128, NKC * D], BF16)
        WT_l = sb("WT_l", [128, NKC * D], BF16)
        WbT_h = sb("WbT_h", [128, NKC * H], BF16)
        WbT_l = sb("WbT_l", [128, NKC * H], BF16)
        AF = sb("AF", [128, 22528], F32)
        AB = sb("AB", [128, 30720], BF16)
        PBK = [S_.enter_context(nc.psum_tensor(f"PB{i}", [128, 512], F32)) for i in range(8)]

        sems, cnt = {}, {}
        for s in ["sLD", "sLD0", "sLD1", "sLD3", "sIN", "sSP", "sP0", "sP1", "sCC", "sV", "sS", "sT", "sG",
                  "sV2", "sS2", "sT2", "sG2", "sQ", "sQS"]:
            sems[s] = S_.enter_context(nc.semaphore(s))
            cnt[s] = 0

        blk = S_.enter_context(nc.Block())
        EV, ES, ET, EG, EY = nc.vector, nc.scalar, nc.tensor, nc.gpsimd, nc.sync

        R = {"on": False}

        def rs(s):
            return s + "2" if (R["on"] and s in ("sT", "sV", "sS", "sG")) else s

        def inc(inst, s, k=1):
            s = rs(s)
            inst.then_inc(sems[s], k)
            cnt[s] += k
            return cnt[s]

        def w(eng, s, val=None):
            s = rs(s)
            eng.wait_ge(sems[s], cnt[s] if val is None else val)

        def vmark():
            EV.drain()
            return inc(EV.nop(), "sV")

        def smark():
            ES.drain()
            return inc(ES.nop(), "sS")

        bank_guard = {}

        def use_bank(i):
            g = bank_guard.pop(i, None)
            if g is not None:
                ET.wait_ge(sems[g[0]], g[1])

        def af(off, n=1024):
            return AF[:, off:off + n]

        def ab(off, n=1024):
            return AB[:, off:off + n]

        def r3(apx, k=16):
            return apx.rearrange("p (b k) -> p b k", k=k)

        def rh3(apx):
            return apx.rearrange("p (h d) -> p h d", h=H)

        def dump(nm, src_ap):
            if not DBG:
                return
            w(EG, "sV")
            w(EG, "sS")
            inc(EG.dma_start(dbg[nm][:, :], src_ap), "sSP", 16)
            w(EG, "sSP")
            w(EV, "sSP")
            w(ES, "sSP")

        qlast = [0]

        def quant(src, F, split=True, qh_off=None, ql_off=None):
            NB = F // 16
            qh_off = bQH if qh_off is None else qh_off
            ql_off = bQL if ql_off is None else ql_off
            pm, pp, pr = AF[:, qM:qM + NB], AF[:, qP:qP + NB], AF[:, qR:qR + NB]
            if qlast[0]:
                EV.wait_ge(sems[qlast[0][0]], qlast[0][1])
            F2, NB2 = F // 2, NB // 2
            sh = [src[:, 0:F2], src[:, F2:F]]

            def hv(off, h):
                return AF[:, off + h * F2:off + (h + 1) * F2]

            def hb(off, h):
                return AB[:, off + h * F2:off + (h + 1) * F2]

            pmh = [pm[:, 0:NB2], pm[:, NB2:NB]]
            pph = [pp[:, 0:NB2], pp[:, NB2:NB]]
            prh = [pr[:, 0:NB2], pr[:, NB2:NB]]
            mk = {}
            for h in (0, 1):
                mk["r", h] = inc(EV.tensor_reduce(
                    pmh[h], r3(sh[h]), axis=AX.X, op=Alu.max,
                    apply_absolute_value=True), "sQ")
            for h in (0, 1):
                EV.wait_ge(sems["sQ"], mk["r", h])
                mk["p", h] = inc(EV.tensor_scalar(
                    pph[h], pmh[h], 1.0 / 12.0, 5e-13,
                    op0=Alu.mult, op1=Alu.max), "sQ")
            for h in (0, 1):
                EV.wait_ge(sems["sQ"], mk["p", h])
                mk["i", h] = inc(EV.reciprocal(prh[h], pph[h]), "sQ")
            u_ok = 0
            for h in (0, 1):
                EV.wait_ge(sems["sQ"], mk["i", h])
                u_ok = inc(EV.tensor_tensor(
                    r3(hv(qU, h)), r3(sh[h]),
                    prh[h][:, :, None].broadcast_to((128, NB2, 16)),
                    op=Alu.mult), "sV")
            w(ES, "sV", u_ok)
            xq_f = inc(ES.activation(af(qXQ, F), af(qU, F), Act.Copy, bias=MAGIC), "sQS")
            ES.activation(af(qC1, F), af(qU, F), Act.Copy, scale=VELT)
            ES.wait_ge(sems["sQS"], xq_f)
            lo_ok = inc(ES.activation(ab(bLO, F), af(qXQ, F), Act.Copy, bias=-MAGIC), "sS")
            w(EV, "sS", lo_ok)
            t1_f = inc(EV.tensor_tensor(af(qT1, F), af(qC1, F), af(qU, F),
                                        op=Alu.subtract), "sQ")
            ab_f = inc(EV.tensor_scalar(af(qXQ, F).bitcast(I32), af(qU, F).bitcast(I32),
                                        0x7FFFFFFF, None, op0=Alu.bitwise_and), "sQ")
            EV.wait_ge(sems["sQ"], t1_f)
            th_f = inc(EV.tensor_tensor(ab(bTH, F), af(qC1, F), af(qT1, F),
                                        op=Alu.subtract), "sQ")
            EV.wait_ge(sems["sQ"], ab_f)
            m_f = inc(EV.tensor_scalar(ab(bM, F), af(qXQ, F), 1.75, None,
                                       op0=Alu.is_gt), "sQ")
            EV.wait_ge(sems["sQ"], th_f)
            s1 = {}
            for h in (0, 1):
                s1[h] = inc(EV.tensor_tensor(hb(bSEL, h), hb(bTH, h), hb(bLO, h),
                                             op=Alu.subtract), "sQ")
            EV.wait_ge(sems["sQ"], m_f)
            s2 = {}
            for h in (0, 1):
                EV.wait_ge(sems["sQ"], s1[h])
                s2[h] = inc(EV.tensor_tensor(hb(bSEL, h), hb(bM, h), hb(bSEL, h),
                                             op=Alu.mult), "sQ")
            sel_ok = 0
            for h in (0, 1):
                EV.wait_ge(sems["sQ"], s2[h])
                sel_ok = inc(EV.tensor_tensor(hb(bSEL, h), hb(bLO, h), hb(bSEL, h),
                                              op=Alu.add), "sV")
            w(EG, "sV", sel_ok)
            xq_ok = inc(EG.tensor_tensor(r3(af(qXQ, F)), r3(ab(bSEL, F)),
                        pp[:, :, None].broadcast_to((128, NB, 16)), op=Alu.mult), "sG")
            if split:
                w(ES, "sG", xq_ok)
                qh_ok = inc(ES.activation(ab(qh_off, F), af(qXQ, F), Act.Copy), "sS")
                w(EG, "sS", qh_ok)
                xq_ok = inc(EG.tensor_tensor(ab(ql_off, F), af(qXQ, F), ab(qh_off, F),
                                             op=Alu.subtract), "sG")
            qlast[0] = (rs("sG"), xq_ok)
            return xq_ok

        def layernorm(src_off, dst_off, sq_off=None):
            sq_off = src_off if sq_off is None else sq_off
            q1 = inc(EV.tensor_reduce(lnm[:], af(src_off), axis=AX.X, op=Alu.add), "sQ")
            EV.wait_ge(sems["sQ"], q1)
            q2 = inc(EV.tensor_scalar(lnm[:], lnm[:], 1.0 / D, None, op0=Alu.mult), "sQ")
            EV.wait_ge(sems["sQ"], q2)
            v1 = inc(EV.tensor_scalar(af(dst_off), af(src_off), lnm[:], None,
                                      op0=Alu.subtract), "sV")
            w(ES, "sV", v1)
            s1 = inc(ES.activation(af(sq_off), af(dst_off), Act.Square), "sS")
            w(EV, "sS", s1)
            q3 = inc(EV.tensor_reduce(lnv[:], af(sq_off), axis=AX.X, op=Alu.add), "sQ")
            EV.wait_ge(sems["sQ"], q3)
            v2 = inc(EV.tensor_scalar(lnv[:], lnv[:], 1.0 / D, 1e-5,
                                      op0=Alu.mult, op1=Alu.add), "sV")
            w(ES, "sV", v2)
            s2 = inc(ES.activation(lnv[:], lnv[:], Act.Sqrt), "sS")
            w(EV, "sS", s2)
            mr = inc(EV.reciprocal(lnv[:], lnv[:]), "sQ")
            EV.wait_ge(sems["sQ"], mr)
            return inc(EV.tensor_scalar(af(dst_off), af(dst_off), lnv[:], None,
                                        op0=Alu.mult), "sV")

        def tr_blocks(src, nblk, bank):
            batches = []
            done = 0
            while done < nblk:
                nb = min(4, nblk - done)
                batches.append((done, nb))
                done += nb
            return batches

        def transpose_to(src, nblk, bank, evac):
            for (j0, nb) in tr_blocks(src, nblk, bank):
                use_bank(bank)
                last = None
                for j in range(nb):
                    last = ET.transpose(PBK[bank][:, j * 128:(j + 1) * 128],
                                        src[:, (j0 + j) * 128:(j0 + j + 1) * 128], idn[:])
                tc = inc(last, "sT")
                evac(j0, nb, tc)

        def evac_split(dsth, dstl, bank):
            def ev(j0, nb, tc):
                ES.wait_ge(sems["sT"], tc)
                sc = inc(ES.activation(dsth[:, j0 * 128:(j0 + nb) * 128],
                                       PBK[bank][:, 0:nb * 128], Act.Copy), "sS")
                EV.wait_ge(sems["sS"], sc)
                vc = inc(EV.tensor_tensor(dstl[:, j0 * 128:(j0 + nb) * 128],
                                          PBK[bank][:, 0:nb * 128],
                                          dsth[:, j0 * 128:(j0 + nb) * 128],
                                          op=Alu.subtract), "sV")
                bank_guard[bank] = (rs("sV"), vc)
            return ev

        def evac_copy(dst, bank):
            def ev(j0, nb, tc):
                EV.wait_ge(sems["sT"], tc)
                vc = inc(EV.tensor_copy(dst[:, j0 * 128:(j0 + nb) * 128],
                                        PBK[bank][:, 0:nb * 128]), "sV")
                bank_guard[bank] = (rs("sV"), vc)
            return ev

        def mm3(bank, ncol, lh, ll, rh, rl):
            last = None
            n = len(lh)
            for i in range(n):
                last = ET.matmul(PBK[bank][:, 0:ncol], lh[i], rh[i],
                                 start=(i == 0), stop=False)
                last = ET.matmul(PBK[bank][:, 0:ncol], lh[i], rl[i],
                                 start=False, stop=False)
                last = ET.matmul(PBK[bank][:, 0:ncol], ll[i], rh[i],
                                 start=False, stop=(i == n - 1))
            return inc(last, "sT")

        inc(EY.dma_start(lgb[:], dec_d[0:1, :].partition_broadcast(128)), "sIN", 16)
        inc(EY.dma_start(smb[:], sm_d[0:1, :].partition_broadcast(128)), "sIN", 16)
        EG.iota(iot[:], [[-1, 128]], base=0, channel_multiplier=1)
        EG.iota(irow[:], [[1, 128]], base=1, channel_multiplier=0)
        EG.drain()
        EG.tensor_scalar(idn[:], iot[:], 0.0, None, op0=Alu.is_equal)
        EG.drain()
        EG.tensor_copy(idnb[:], idn[:])
        EG.tensor_copy(idf[:], iot[:])
        EG.tensor_copy(irf[:], irow[:])
        EG.drain()
        g0 = inc(EG.nop(), "sV")
        w(EV, "sIN")
        EV.tensor_scalar(ismb[:], smb[:], -1.0, 1.0, op0=Alu.mult, op1=Alu.add)
        EV.drain()
        w(ES, "sIN")
        ES.activation(lgb[:], lgb[:], Act.Sigmoid)
        ES.drain()
        ES.activation(lgb[:], lgb[:], Act.Ln)
        lg_ok = smark()
        w(EV, "sV", g0)
        EV.tensor_scalar(mti[:], idf[:], -0.5, None, op0=Alu.is_gt)
        EV.tensor_scalar(mts[:], idf[:], 0.5, None, op0=Alu.is_gt)
        w(EV, "sS", lg_ok)
        EV.drain()
        EV.tensor_tensor(rh3(Lm[:]), idf[:, None, :].broadcast_to((128, H, 128)),
                         lgb[:, :, None].broadcast_to((128, H, 128)), op=Alu.mult)
        EV.drain()
        EV.tensor_tensor(rh3(de_f[:]), irf[:, None, :].broadcast_to((128, H, 128)),
                         lgb[:, :, None].broadcast_to((128, H, 128)), op=Alu.mult)
        v1 = vmark()
        w(ES, "sV", v1)
        ES.activation(Lm[:], Lm[:], Act.Exp)
        ES.activation(de_f[:], de_f[:], Act.Exp)
        s1 = smark()
        w(EV, "sS", s1)
        EV.tensor_tensor(rh3(Lm[:]), rh3(Lm[:]),
                         mti[:, None, :].broadcast_to((128, H, 128)), op=Alu.mult)
        EV.drain()
        EV.tensor_tensor(rh3(nLs[:]), rh3(Lm[:]),
                         mts[:, None, :].broadcast_to((128, H, 128)), op=Alu.mult)
        EV.drain()
        EV.tensor_scalar(nLs[:], nLs[:], -1.0, None, op0=Alu.mult)
        EV.tensor_scalar(colv[:], idf[:, 0:1], 1.0, None, op0=Alu.add)
        EV.drain()
        EV.tensor_tensor(de_c[:], colv[:].broadcast_to((128, H)), lgb[:, 0:H], op=Alu.mult)
        EV.drain()
        EV.tensor_scalar(colv[:], idf[:, 0:1], -1.0, float(C - 1), op0=Alu.mult, op1=Alu.add)
        EV.drain()
        EV.tensor_tensor(dw_c[:], colv[:].broadcast_to((128, H)), lgb[:, 0:H], op=Alu.mult)
        EV.drain()
        EV.memset(colv[:], float(C))
        EV.drain()
        EV.tensor_tensor(ctd_c[:], colv[:].broadcast_to((128, H)), lgb[:, 0:H], op=Alu.mult)
        v2 = vmark()
        w(ES, "sV", v2)
        ES.activation(de_c[:], de_c[:], Act.Exp)
        ES.activation(dw_c[:], dw_c[:], Act.Exp)
        ES.activation(ctd_c[:], ctd_c[:], Act.Exp)
        tables_ok_s = smark()
        EV.memset(Smat[:], 0.0)
        smat0 = vmark()
        if DBG:
            dump("dwc", dw_c[:])
            dump("dec2", de_c[:])
            dump("ctdc", ctd_c[:])
            dump("Lmt", Lm[:])
            dump("nLst", nLs[:])
            dump("mtit", mti[:])

        wq_slices = []
        for (src, dh, dl, rows, colsF) in [
            (wg_d, swg[0], swg[1], IF, D),
            (wu_d, swu[0], swu[1], IF, D),
            (wd_d, swd[0], swd[1], D, IF),
        ]:
            for r0 in range(0, rows, 128):
                for f0 in range(0, colsF, 1024):
                    wq_slices.append((src, dh, dl, r0, f0, min(1024, colsF - f0)))
        wq_state = {"j": 0, "buf_free": [0, 0], "spill": [0, 0], "t0": None}

        def pump_wq(nj=1):
            for _ in range(nj):
                j = wq_state["j"]
                if j >= len(wq_slices):
                    return
                if wq_state["t0"] is None:
                    wq_state["t0"] = cnt["sT"]
                wq_state["j"] = j + 1
                (src, dh, dl, r0, f0, F) = wq_slices[j]
                par = j % 2
                lsem = "sLD0" if par == 0 else "sLD1"
                psem = "sP0" if par == 0 else "sP1"
                qh_o = 4096 if par == 0 else 8192
                ql_o = 5120 if par == 0 else 9216
                if wq_state["buf_free"][par]:
                    EY.wait_ge(sems["sV"], wq_state["buf_free"][par])
                else:
                    EY.wait_ge(sems["sT"], wq_state["t0"])
                inc(EY.dma_start(af(wLD[par], F), src[r0:r0 + 128, f0:f0 + F]), lsem, 16)
                w(EV, lsem)
                if wq_state["spill"][par]:
                    w(ES, psem, wq_state["spill"][par])
                    w(EG, psem, wq_state["spill"][par])
                quant(af(wLD[par], F), F, split=True, qh_off=qh_o, ql_off=ql_o)
                w(EG, "sG")
                inc(EG.dma_start(dh[r0:r0 + 128, f0:f0 + F], ab(qh_o, F)), psem, 16)
                inc(EG.dma_start(dl[r0:r0 + 128, f0:f0 + F], ab(ql_o, F)), psem, 16)
                wq_state["spill"][par] = cnt[psem]
                wq_state["buf_free"][par] = cnt["sV"]

        for (Wsrc, spill) in [(Wr_d, True), (Ww_d, False)]:
            for r0 in range(0, D, 128):
                EY.wait_ge(sems["sT"], cnt["sT"])
                inc(EY.dma_start(af(wLD[0], D), Wsrc[r0:r0 + 128, :]), "sLD", 16)
                w(ET, "sLD")

                def mk(dh_, dl_, r0_):
                    def ev(j0, nb, tc):
                        ES.wait_ge(sems["sT"], tc)
                        sc = 0
                        for jj in range(nb):
                            kc = j0 + jj
                            sc = inc(ES.activation(dh_[:, kc * D + r0_:kc * D + r0_ + 128],
                                                   PBK[3][:, jj * 128:(jj + 1) * 128],
                                                   Act.Copy), "sS")
                        EV.wait_ge(sems["sS"], sc)
                        vc = 0
                        for jj in range(nb):
                            kc = j0 + jj
                            vc = inc(EV.tensor_tensor(
                                dl_[:, kc * D + r0_:kc * D + r0_ + 128],
                                PBK[3][:, jj * 128:(jj + 1) * 128],
                                dh_[:, kc * D + r0_:kc * D + r0_ + 128],
                                op=Alu.subtract), "sV")
                        bank_guard[3] = ("sV", vc)
                    return ev
                transpose_to(af(wLD[0], D), 8, 3, mk(WT_h, WT_l, r0))
            if spill:
                w(EG, "sV")
                inc(EG.dma_start(swt[0][:, :], WT_h[:, :]), "sSP", 16)
                inc(EG.dma_start(swt[1][:, :], WT_l[:, :]), "sSP", 16)
                w(ES, "sSP")
                w(EV, "sSP")
        EY.wait_ge(sems["sT"], cnt["sT"])
        inc(EY.dma_start(AF[0:H, wLD[0]:wLD[0] + D], Wb_d[:, :]), "sLD", 16)
        w(ET, "sLD")
        use_bank(3)
        last = None
        for kc in range(8):
            last = ET.transpose(PBK[3][:, kc * 8:kc * 8 + 8],
                                AF[0:H, wLD[0] + kc * 128:wLD[0] + (kc + 1) * 128],
                                idn[0:H, 0:H])
        tc = inc(last, "sT")
        ES.wait_ge(sems["sT"], tc)
        ES.activation(WbT_h[:, :], PBK[3][:, 0:H * NKC], Act.Copy)
        sc = smark()
        EV.wait_ge(sems["sS"], sc)
        vc = inc(EV.tensor_tensor(WbT_l[:, :], PBK[3][:, 0:H * NKC], WbT_h[:, :],
                                  op=Alu.subtract), "sV")
        bank_guard[3] = ("sV", vc)

        mx_guard = [0]
        ck_guard = [0]
        fl_state = {"k_ok": 0}

        def front_lite(x_dram, row0):
            if mx_guard[0]:
                EY.wait_ge(sems["sV"], mx_guard[0])
            inc(EY.dma_start(af(mXT), x_dram[row0:row0 + 128, :]), "sLD3", 16)
            yield
            w(EV, "sLD3")
            m1 = inc(EV.tensor_reduce(lnm[:], af(mXT), axis=AX.X, op=Alu.add), "sQ")
            EV.wait_ge(sems["sQ"], m1)
            m2 = inc(EV.tensor_scalar(lnm[:], lnm[:], 1.0 / D, None, op0=Alu.mult), "sQ")
            EV.wait_ge(sems["sQ"], m2)
            m3 = inc(EV.tensor_scalar(af(mXN), af(mXT), lnm[:], None,
                                      op0=Alu.subtract), "sQ")
            yield
            EV.wait_ge(sems["sQ"], m3)
            m4 = inc(EV.tensor_tensor(af(mXT), af(mXN), af(mXN), op=Alu.mult), "sQ")
            EV.wait_ge(sems["sQ"], m4)
            m5 = inc(EV.tensor_reduce(lnv[:], af(mXT), axis=AX.X, op=Alu.add), "sQ")
            EV.wait_ge(sems["sQ"], m5)
            v2 = inc(EV.tensor_scalar(lnv[:], lnv[:], 1.0 / D, 1e-5,
                                      op0=Alu.mult, op1=Alu.add), "sV")
            yield
            w(ES, "sV", v2)
            s2 = inc(ES.activation(lnv[:], lnv[:], Act.Sqrt), "sS")
            w(EV, "sS", s2)
            m6 = inc(EV.reciprocal(lnv[:], lnv[:]), "sQ")
            EV.wait_ge(sems["sQ"], m6)
            m7 = inc(EV.tensor_scalar(af(mXN), af(mXN), lnv[:], None,
                                      op0=Alu.mult), "sV")
            yield
            w(EV, "sV", m7)
            m8 = inc(EV.tensor_tensor(af(mXT), af(mXN), af(mXN), op=Alu.mult), "sQ")
            EV.wait_ge(sems["sQ"], m8)
            v1 = inc(EV.tensor_reduce(rn[:], rh3(af(mXT)), axis=AX.X, op=Alu.add), "sV")
            mx_guard[0] = v1
            yield
            w(ES, "sV", v1)
            s3 = inc(ES.activation(rn[:], rn[:], Act.Sqrt), "sS")
            w(EV, "sS", s3)
            m9 = inc(EV.tensor_scalar(rn[:], rn[:], 1e-12, None, op0=Alu.max), "sQ")
            EV.wait_ge(sems["sQ"], m9)
            m10 = inc(EV.reciprocal(rn[:], rn[:]), "sQ")
            yield
            EV.wait_ge(sems["sQ"], m10)
            fl_state["k_ok"] = inc(EV.tensor_tensor(
                rh3(af(mKT)), rh3(af(mXN)),
                rn[:, :, None].broadcast_to((128, H, HD)), op=Alu.mult), "sV")

        def mixer_chunk(row0, with_o, o_dram, fl_next=None, spill=False, dmp=False):
            if spill and ck_guard[0]:
                w(EV, "sSP", ck_guard[0])
            k_ok = fl_state["k_ok"]
            w(ET, "sV")
            transpose_to(af(mXN), 8, 0, evac_split(AB[:, bXFH:bXFH + D],
                                                   AB[:, bXFL:bXFL + D], 0))
            if dmp:
                dump("xn", af(mXN))
                dump("kt", af(mKT))

            lh = [AB[:, bXFH + kc * 128:bXFH + (kc + 1) * 128] for kc in range(8)]
            ll = [AB[:, bXFL + kc * 128:bXFL + (kc + 1) * 128] for kc in range(8)]
            w(ET, "sV")
            vtc = []
            for pan in (0, 1):
                use_bank(1 + pan)
                rh = [WT_h[:, kc * D + pan * 512:kc * D + pan * 512 + 512] for kc in range(8)]
                rl = [WT_l[:, kc * D + pan * 512:kc * D + pan * 512 + 512] for kc in range(8)]
                vtc.append(mm3(1 + pan, 512, lh, ll, rh, rl))
            use_bank(7)
            rbh = [WbT_h[:, kc * H:(kc + 1) * H] for kc in range(8)]
            rbl = [WbT_l[:, kc * H:(kc + 1) * H] for kc in range(8)]
            btc = mm3(7, H, lh, ll, rbh, rbl)
            ES.wait_ge(sems["sT"], btc)
            bet_ok = inc(ES.activation(bet[:], PBK[7][:, 0:H], Act.Sigmoid), "sS")
            bank_guard[7] = ("sS", bet_ok)
            EV.wait_ge(sems["sS"], bet_ok)
            rrv = AF[:, mRR:mRR + 2048].rearrange("p (h t) -> p h t", h=H)
            for pan in (0, 1):
                EV.tensor_tensor(rrv[:, pan * 4:(pan + 1) * 4, 0:128],
                                 PBK[1 + pan][:, 0:512].rearrange("p (h d) -> p h d", h=4),
                                 bet[:, pan * 4:(pan + 1) * 4, None].broadcast_to((128, 4, HD)),
                                 op=Alu.mult)
            vkb = inc(EV.tensor_tensor(rh3(af(mKBT)), rh3(af(mKT)),
                                       bet[:, :, None].broadcast_to((128, H, HD)),
                                       op=Alu.mult), "sV")
            bank_guard[1] = ("sV", vkb)
            bank_guard[2] = ("sV", vkb)
            EV.tensor_tensor(rh3(af(mKDW)), rh3(af(mKT)),
                             dw_c[:, :, None].broadcast_to((128, H, HD)), op=Alu.mult)
            EV.wait_ge(sems["sV"], vkb)
            rr_ok = inc(EV.tensor_tensor(rrv[:, :, 128:256], rh3(af(mKBT)),
                                         de_c[:, :, None].broadcast_to((128, H, HD)),
                                         op=Alu.mult), "sV")
            if dmp:
                dump("bet", bet[:])
                dump("R0", AF[:, mRR:mRR + 2048])
                dump("kdw", af(mKDW))

            w(ET, "sV", rr_ok)
            transpose_to(af(mKT), 8, 0, evac_copy(af(mKFM), 0))
            transpose_to(af(mKBT), 8, 0, evac_copy(af(mKBFM), 0))
            w(EV, "sV")
            kdef_ok = inc(EV.tensor_tensor(af(mKDEF), af(mKFM), de_f[:, :],
                                           op=Alu.mult), "sV")

            kf = [af(mKFM)[:, h * 128:(h + 1) * 128] for h in range(H)]
            kbf = [af(mKBFM)[:, h * 128:(h + 1) * 128] for h in range(H)]
            w(ET, "sV", kdef_ok)
            for grp in (0, 1):
                use_bank(3 + grp)
            last = None
            for h in range(H):
                last = ET.matmul(PBK[3 + h // 4][:, (h % 4) * 128:(h % 4) * 128 + 128],
                                 kbf[h], kf[h], start=True, stop=True)
            tc = inc(last, "sT")
            EV.wait_ge(sems["sT"], tc)
            vc = 0
            for grp in (0, 1):
                vc = inc(EV.tensor_tensor(AF[:, mPP + grp * 512:mPP + (grp + 1) * 512],
                                          PBK[3 + grp][:, 0:512],
                                          nLs[:, grp * 512:(grp + 1) * 512],
                                          op=Alu.mult), "sV")
            bank_guard[3] = ("sV", vc)
            bank_guard[4] = ("sV", vc)
            if dmp:
                dump("A0", AF[:, mPP:mPP + 1024])
                dump("kdef", af(mKDEF))
            if True:
                w(ET, "sV", vc)
                for grp in (0, 1):
                    use_bank(3 + grp)
                last = None
                for h in range(H):
                    last = ET.matmul(PBK[3 + h // 4][:, (h % 4) * 128:(h % 4) * 128 + 128],
                                     kf[h], kf[h], start=True, stop=True)
                tc = inc(last, "sT")
                EV.wait_ge(sems["sT"], tc)
                vc = 0
                for grp in (0, 1):
                    vc = inc(EV.tensor_tensor(
                        AF[:, mATT + grp * 512:mATT + (grp + 1) * 512],
                        PBK[3 + grp][:, 0:512],
                        Lm[:, grp * 512:(grp + 1) * 512], op=Alu.mult), "sV")
                bank_guard[3] = ("sV", vc)
                bank_guard[4] = ("sV", vc)

            w(ET, "sV")
            for (j0, nb) in ((0, 4), (4, 4)):
                use_bank(0)
                last = None
                for j in range(nb):
                    h = j0 + j
                    last = ET.transpose(PBK[0][:, j * 128:(j + 1) * 128],
                                        AF[:, mPP + h * 128:mPP + (h + 1) * 128], idn[:])
                tc = inc(last, "sT")
                ES.wait_ge(sems["sT"], tc)
                ii = ES.activation(AF[:, mPPT + j0 * 128:mPPT + (j0 + nb) * 128],
                                   PBK[0][:, 0:nb * 128], Act.Copy)
                sc = inc(ii, "sS")
                bank_guard[0] = ("sS", sc)
            pp_ok = sc
            rr_mark = 0
            ppt = [af(mPPT)[:, h * 128:(h + 1) * 128] for h in range(H)]
            ppn = [af(mPP)[:, h * 128:(h + 1) * 128] for h in range(H)]
            for lev in range(7):
                if fl_next is not None:
                    next(fl_next, None)
                ET.wait_ge(sems["sS"], pp_ok)
                for b in (1, 2, 5, 6):
                    use_bank(b)
                last = None
                for h in range(H):
                    bnk = [1, 2, 5, 6][h // 2]
                    col = (h % 2) * 256
                    last = ET.matmul(PBK[bnk][:, col:col + 256], ppt[h],
                                     AF[:, mRR + h * 256:mRR + (h + 1) * 256],
                                     start=True, stop=True)
                tc_ap = inc(last, "sT")
                if lev < 6:
                    for b in (3, 4, 0, 7):
                        use_bank(b)
                    last = None
                    for h in range(H):
                        last = ET.matmul(PBK[3 + h // 4][:, (h % 4) * 128:(h % 4) * 128 + 128],
                                         ppt[h], ppn[h], start=True, stop=True)
                    for h in range(H):
                        last = ET.matmul(PBK[(0, 7)[h // 4]][:, (h % 4) * 128:(h % 4) * 128 + 128],
                                         ppn[h], ppt[h], start=True, stop=True)
                    tc_sq = inc(last, "sT")
                    ES.wait_ge(sems["sT"], tc_sq)
                    ES.activation(AF[:, mPP:mPP + 512], PBK[3][:, 0:512], Act.Copy)
                    ES.activation(AF[:, mPP + 512:mPP + 1024], PBK[4][:, 0:512], Act.Copy)
                    ES.activation(AF[:, mPPT:mPPT + 512], PBK[0][:, 0:512], Act.Copy)
                    ii = ES.activation(AF[:, mPPT + 512:mPPT + 1024], PBK[7][:, 0:512],
                                       Act.Copy)
                    pp_ok = inc(ii, "sS")
                    for b in (3, 4, 0, 7):
                        bank_guard[b] = ("sS", pp_ok)
                EV.wait_ge(sems["sT"], tc_ap)
                if rr_mark:
                    EV.wait_ge(sems["sV"], rr_mark)
                last = None
                for bq in range(4):
                    bnk = [1, 2, 5, 6][bq]
                    last = EV.tensor_tensor(AF[:, mRR + bq * 512:mRR + (bq + 1) * 512],
                                            AF[:, mRR + bq * 512:mRR + (bq + 1) * 512],
                                            PBK[bnk][:, 0:512], op=Alu.add)
                vc = inc(last, "sV")
                rr_mark = vc
                for bq in range(4):
                    bank_guard[[1, 2, 5, 6][bq]] = ("sV", vc)
            w(EV, "sV", rr_mark)

            w(ET, "sV")
            for grp in (0, 1):
                use_bank(3 + grp)
            last = None
            for h in range(H):
                last = ET.transpose(PBK[3 + h // 4][:, (h % 4) * 128:(h % 4) * 128 + 128],
                                    AF[:, mRR + h * 256 + 128:mRR + h * 256 + 256], idn[:])
            tc = inc(last, "sT")
            EV.wait_ge(sems["sT"], tc)
            vc = 0
            for grp in (0, 1):
                vc = inc(EV.tensor_copy(AF[:, mKCDT + grp * 512:mKCDT + (grp + 1) * 512],
                                        PBK[3 + grp][:, 0:512]), "sV")
            bank_guard[3] = ("sV", vc)
            bank_guard[4] = ("sV", vc)
            w(ET, "sV", vc)
            for grp in (0, 1):
                use_bank(3 + grp)
            last = None
            for h in range(H):
                last = ET.matmul(PBK[3 + h // 4][:, (h % 4) * 128:(h % 4) * 128 + 128],
                                 AF[:, mKCDT + h * 128:mKCDT + (h + 1) * 128],
                                 Smat[:, h * 128:(h + 1) * 128], start=True, stop=True)
            tc = inc(last, "sT")
            EV.wait_ge(sems["sT"], tc)
            rrv = AF[:, mRR:mRR + 2048].rearrange("p (h t) -> p h t", h=H)
            for grp in (0, 1):
                vnew_ok = inc(EV.scalar_tensor_tensor(
                    af(mVNEW)[:, grp * 512:(grp + 1) * 512].rearrange("p (h d) -> p h d", h=4),
                    PBK[3 + grp][:, 0:512].rearrange("p (h d) -> p h d", h=4), -1.0,
                    rrv[:, grp * 4:(grp + 1) * 4, 0:128],
                    op0=Alu.mult, op1=Alu.add), "sV")
            bank_guard[3] = ("sV", vnew_ok)
            bank_guard[4] = ("sV", vnew_ok)

            if dmp:
                dump("vnew", af(mVNEW))
                if with_o:
                    dump("att", AF[:, mATT:mATT + 1024])
            w(ET, "sV", vnew_ok)
            for grp in (0, 1):
                use_bank(5 + grp)
            last = None
            for h in range(H):
                last = ET.transpose(PBK[5 + h // 4][:, (h % 4) * 128:(h % 4) * 128 + 128],
                                    AF[:, mATT + h * 128:mATT + (h + 1) * 128], idn[:])
            tc = inc(last, "sT")
            EV.wait_ge(sems["sT"], tc)
            attT_ok = 0
            for grp in (0, 1):
                attT_ok = inc(EV.tensor_copy(
                    AF[:, mATTT + grp * 512:mATTT + (grp + 1) * 512],
                    PBK[5 + grp][:, 0:512]), "sV")
            bank_guard[5] = ("sV", attT_ok)
            bank_guard[6] = ("sV", attT_ok)
            if with_o:
                w(ET, "sV", attT_ok)
                for grp in (0, 1):
                    use_bank(5 + grp)
                last = None
                for h in range(H):
                    bnk = 5 + h // 4
                    col = (h % 4) * 128
                    ET.matmul(PBK[bnk][:, col:col + 128],
                              af(mKDEF)[:, h * 128:(h + 1) * 128],
                              Smat[:, h * 128:(h + 1) * 128], start=True, stop=False)
                    last = ET.matmul(PBK[bnk][:, col:col + 128],
                                     AF[:, mATTT + h * 128:mATTT + (h + 1) * 128],
                                     af(mVNEW)[:, h * 128:(h + 1) * 128],
                                     start=False, stop=True)
                tc = inc(last, "sT")
                EV.wait_ge(sems["sT"], tc)
                w(EV, "sSP")
                vc = 0
                for grp in (0, 1):
                    vc = inc(EV.tensor_copy(af(mOT)[:, grp * 512:(grp + 1) * 512],
                                            PBK[5 + grp][:, 0:512]), "sV")
                bank_guard[5] = ("sV", vc)
                bank_guard[6] = ("sV", vc)
                w(EG, "sV", vc)
                inc(EG.dma_start(o_dram[row0:row0 + 128, :], af(mOT)), "sSP", 16)

            w(ET, "sV", vnew_ok)
            for grp in (0, 1):
                use_bank(3 + grp)
            last = None
            for h in range(H):
                last = ET.matmul(PBK[3 + h // 4][:, (h % 4) * 128:(h % 4) * 128 + 128],
                                 af(mKDW)[:, h * 128:(h + 1) * 128],
                                 af(mVNEW)[:, h * 128:(h + 1) * 128], start=True, stop=True)
            tc = inc(last, "sT")
            EV.wait_ge(sems["sT"], tc)
            ms = inc(EV.tensor_tensor(rh3(Smat[:]), rh3(Smat[:]),
                                      ctd_c[:, :, None].broadcast_to((128, H, HD)),
                                      op=Alu.mult), "sQ")
            EV.wait_ge(sems["sQ"], ms)
            vc = 0
            for grp in (0, 1):
                vc = inc(EV.tensor_tensor(Smat[:, grp * 512:(grp + 1) * 512],
                                          Smat[:, grp * 512:(grp + 1) * 512],
                                          PBK[3 + grp][:, 0:512], op=Alu.add), "sV")
            bank_guard[3] = ("sV", vc)
            bank_guard[4] = ("sV", vc)
            if spill:
                w(EG, "sV", attT_ok)
                inc(EG.dma_start(sck_d[row0:row0 + 128, :], af(mSPILL, mSPILLW)),
                    "sSP", 16)
                ck_guard[0] = cnt["sSP"]

        fl = front_lite(xo_d, 0)
        for _ in fl:
            pass
        for n in range(NCH):
            pump_wq(4)
            fl_next = front_lite(xo_d, (n + 1) * C) if n + 1 < NCH else None
            mixer_chunk(n * C, False, None, fl_next=fl_next, spill=True,
                        dmp=(DBG and n == 0))
            if fl_next is not None:
                for _ in fl_next:
                    pass

        BK4 = [1, 2, 7, 0]
        spill_guard = {}

        def xqv(gp, piece, kc, tt, n=128):
            base = (8192 if piece == 0 else 12288) + kc * 256 + tt * 64
            return AF[:, base:base + n // 2].bitcast(BF16)

        xq_ok = [0, 0]
        xq_free = [0, 0]

        def prep_group(g4):
            gp = (g4 // 4) % 2
            ntile = min(4, NT - g4)
            for tt in range(ntile):
                tile = g4 + tt
                w(EY, "sV")
                inc(EY.dma_start(af(fXT), xo_d[tile * 128:(tile + 1) * 128, :]), "sLD3", 16)
                w(EV, "sLD3")
                lv = layernorm(fXT, fXN)
                w(ES, "sV")
                w(EG, "sV")
                w(EV, "sV", lv)
                quant(af(fXN), D, split=False)
                w(ET, "sV")
                w(ET, "sG")
                if xq_free[gp]:
                    ET.wait_ge(sems[rs("sT")], xq_free[gp])

                def evx(tt_):
                    def ev(j0, nb, tc):
                        ES.wait_ge(sems[rs("sT")], tc)
                        sc = 0
                        for j in range(nb):
                            kc = j0 + j
                            sc = inc(ES.activation(xqv(gp, 0, kc, tt_),
                                                   PBK[0][:, j * 128:(j + 1) * 128],
                                                   Act.Copy), "sS")
                        EV.wait_ge(sems[rs("sS")], sc)
                        vc = 0
                        for j in range(nb):
                            kc = j0 + j
                            vc = inc(EV.tensor_tensor(xqv(gp, 1, kc, tt_),
                                                      PBK[0][:, j * 128:(j + 1) * 128],
                                                      xqv(gp, 0, kc, tt_),
                                                      op=Alu.subtract), "sV")
                        bank_guard[0] = (rs("sV"), vc)
                    return ev
                transpose_to(af(qXQ), 8, 0, evx(tt))
                yield
            xq_ok[gp] = cnt[rs("sV")]

        def h_chain(g4, tt, pw, po, up_tc_tt):
            tile = g4 + tt
            EV.wait_ge(sems[rs("sT")], up_tc_tt)
            vc = inc(EV.tensor_tensor(AF[:, fHT + tt * 512:fHT + tt * 512 + pw],
                                      AF[:, fHT + tt * 512:fHT + tt * 512 + pw],
                                      PBK[BK4[tt]][:, 0:pw], op=Alu.mult), "sV")
            bank_guard[BK4[tt]] = (rs("sV"), vc)
            w(EV, "sV", vc)
            par = tt % 2
            psem = "sP0" if par == 0 else "sP1"
            qh_o, ql_o = bQH + par * 512, bQL + par * 512
            if par in spill_guard:
                w(ES, psem, spill_guard[par])
                w(EG, psem, spill_guard[par])
            w(ES, "sV", vc)
            w(EG, "sV", vc)
            quant(AF[:, fHT + tt * 512:fHT + tt * 512 + pw], pw, split=True,
                  qh_off=qh_o, ql_off=ql_o)
            w(EG, "sG")
            inc(EG.dma_start(shq[0][tile * 128:(tile + 1) * 128, po:po + pw],
                             ab(qh_o, pw)), psem, 16)
            inc(EG.dma_start(shq[1][tile * 128:(tile + 1) * 128, po:po + pw],
                             ab(ql_o, pw)), psem, 16)
            spill_guard[par] = cnt[psem]

        def p3a_gen():
            yield from prep_group(0)
            wave_guard = [0, 0]
            for g4 in range(0, NT, 4):
                gp = (g4 // 4) % 2
                ntile = min(4, NT - g4)
                for pan in range(6):
                    pw, po = PANW[pan], PANO[pan]
                    for wi, scrw in [(0, swg), (1, swu)]:
                        wv = (pan * 2 + wi) % 2
                        wsem = "sLD0" if wv == 0 else "sLD1"
                        base = bWP[wv]
                        EY.wait_ge(sems[rs("sT")], wave_guard[wv])
                        for kc in range(8):
                            inc(EY.dma_start_transpose(
                                AB[:, base + kc * 512:base + kc * 512 + pw],
                                scrw[0][po:po + pw, kc * 128:(kc + 1) * 128]), wsem, 16)
                            inc(EY.dma_start_transpose(
                                AB[:, base + 4096 + kc * 512:base + 4096 + kc * 512 + pw],
                                scrw[1][po:po + pw, kc * 128:(kc + 1) * 128]), wsem, 16)
                        w(ET, wsem)
                        if wi == 0:
                            w(ET, "sV", xq_ok[gp])
                        rh = [AB[:, base + kc * 512:base + kc * 512 + pw] for kc in range(8)]
                        rl = [AB[:, base + 4096 + kc * 512:base + 4096 + kc * 512 + pw]
                              for kc in range(8)]
                        up_tc = []
                        for tt in range(ntile):
                            lh = [xqv(gp, 0, kc, tt) for kc in range(8)]
                            ll = [xqv(gp, 1, kc, tt) for kc in range(8)]
                            use_bank(BK4[tt])
                            tc = mm3(BK4[tt], pw, lh, ll, rh, rl)
                            if wi == 0:
                                ES.wait_ge(sems[rs("sT")], tc)
                                par = tt % 2
                                if par in spill_guard:
                                    w(ES, "sP0" if par == 0 else "sP1",
                                      spill_guard[par])
                                sc = inc(ES.activation(
                                    AF[:, fHT + tt * 512:fHT + tt * 512 + pw],
                                    PBK[BK4[tt]][:, 0:pw], Act.Silu), "sS")
                                bank_guard[BK4[tt]] = (rs("sS"), sc)
                            else:
                                up_tc.append(tc)
                                if tt >= 2:
                                    h_chain(g4, tt - 2, pw, po, up_tc[tt - 2])
                        wave_guard[wv] = cnt[rs("sT")]
                        if wi == 1:
                            for tt in range(max(0, ntile - 2), ntile):
                                h_chain(g4, tt, pw, po, up_tc[tt])
                        yield
                if pan == 5 and g4 + 4 < NT:
                    xq_free[0] = cnt[rs("sT")]
                    yield from prep_group(g4 + 4)

        def p3a_redirected():
            gen = p3a_gen()
            while True:
                R["on"] = True
                try:
                    next(gen)
                except StopIteration:
                    R["on"] = False
                    return
                R["on"] = False
                yield

        p3a = p3a_redirected()

        pump_wq(len(wq_slices))
        spill_guard[0] = cnt["sP0"]
        spill_guard[1] = cnt["sP1"]
        EY.wait_ge(sems["sP0"], cnt["sP0"])
        EY.wait_ge(sems["sP1"], cnt["sP1"])
        EY.wait_ge(sems["sSP"], cnt["sSP"])
        w(EV, "sIN")
        xc_ok = inc(EV.tensor_tensor(af(mPP, H * 128), Smat[:],
                                     ismb[:].broadcast_to((128, H * 128)),
                                     op=Alu.mult), "sV")
        w(EG, "sV", xc_ok)
        inc(EG.dma_start(ccin_t.ap(), af(mPP, H * 128)), "sCC", 16)
        w(EG, "sCC")
        cc = inc(EG.collective_compute(
            "AllReduce", Alu.add,
            replica_groups=[[i, i + 1] for i in range(0, n_cores, 2)],
            ins=[ccin_t.ap().opt()],
            outs=[ccout_t.ap().opt()]), "sG")
        EG.wait_ge(sems["sG"], cc)
        inc(EG.dma_start(af(mPP, H * 128), ccout_t.ap()), "sCC", 16)
        w(EG, "sCC")
        g_ok = inc(EG.nop(), "sG")
        w(EV, "sG", g_ok)
        inc(EV.tensor_tensor(Smat[:], af(mPP, H * 128),
                             smb[:].broadcast_to((128, H * 128)), op=Alu.mult), "sV")

        for n in range(NCH):
            next(p3a, None)
            next(p3a, None)
            row0 = n * C
            EY.wait_ge(sems["sT"], cnt["sT"])
            EY.wait_ge(sems["sV"], cnt["sV"])
            inc(EY.dma_start(af(mSPILL, mSPILLW), sck_d[row0:row0 + 128, :]), "sLD", 16)
            w(ET, "sLD")
            for grp in (0, 1):
                use_bank(3 + grp)
            last = None
            for h in range(H):
                last = ET.transpose(PBK[3 + h // 4][:, (h % 4) * 128:(h % 4) * 128 + 128],
                                    AF[:, mRR + h * 256 + 128:mRR + h * 256 + 256], idn[:])
            tc = inc(last, "sT")
            EV.wait_ge(sems["sT"], tc)
            vc = 0
            for grp in (0, 1):
                vc = inc(EV.tensor_copy(AF[:, mKCDT + grp * 512:mKCDT + (grp + 1) * 512],
                                        PBK[3 + grp][:, 0:512]), "sV")
            bank_guard[3] = ("sV", vc)
            bank_guard[4] = ("sV", vc)
            w(ET, "sV", vc)
            for grp in (0, 1):
                use_bank(3 + grp)
            last = None
            for h in range(H):
                last = ET.matmul(PBK[3 + h // 4][:, (h % 4) * 128:(h % 4) * 128 + 128],
                                 AF[:, mKCDT + h * 128:mKCDT + (h + 1) * 128],
                                 Smat[:, h * 128:(h + 1) * 128], start=True, stop=True)
            tc = inc(last, "sT")
            EV.wait_ge(sems["sT"], tc)
            rrv = AF[:, mRR:mRR + 2048].rearrange("p (h t) -> p h t", h=H)
            for grp in (0, 1):
                vnew_ok = inc(EV.scalar_tensor_tensor(
                    af(mVNEW)[:, grp * 512:(grp + 1) * 512].rearrange("p (h d) -> p h d", h=4),
                    PBK[3 + grp][:, 0:512].rearrange("p (h d) -> p h d", h=4), -1.0,
                    rrv[:, grp * 4:(grp + 1) * 4, 0:128],
                    op0=Alu.mult, op1=Alu.add), "sV")
            bank_guard[3] = ("sV", vnew_ok)
            bank_guard[4] = ("sV", vnew_ok)
            w(ET, "sV", vnew_ok)
            for grp in (0, 1):
                use_bank(5 + grp)
            last = None
            for h in range(H):
                bnk = 5 + h // 4
                col = (h % 4) * 128
                ET.matmul(PBK[bnk][:, col:col + 128],
                          af(mKDEF)[:, h * 128:(h + 1) * 128],
                          Smat[:, h * 128:(h + 1) * 128], start=True, stop=False)
                last = ET.matmul(PBK[bnk][:, col:col + 128],
                                 AF[:, mATTT + h * 128:mATTT + (h + 1) * 128],
                                 af(mVNEW)[:, h * 128:(h + 1) * 128],
                                 start=False, stop=True)
            tc = inc(last, "sT")
            EV.wait_ge(sems["sT"], tc)
            w(EV, "sSP")
            vc = 0
            for grp in (0, 1):
                vc = inc(EV.tensor_copy(af(mOT)[:, grp * 512:(grp + 1) * 512],
                                        PBK[5 + grp][:, 0:512]), "sV")
            bank_guard[5] = ("sV", vc)
            bank_guard[6] = ("sV", vc)
            w(EG, "sV", vc)
            inc(EG.dma_start(so_d[row0:row0 + 128, :], af(mOT)), "sSP", 16)
            w(ET, "sV", vnew_ok)
            for grp in (0, 1):
                use_bank(3 + grp)
            last = None
            for h in range(H):
                last = ET.matmul(PBK[3 + h // 4][:, (h % 4) * 128:(h % 4) * 128 + 128],
                                 af(mKDW)[:, h * 128:(h + 1) * 128],
                                 af(mVNEW)[:, h * 128:(h + 1) * 128], start=True, stop=True)
            tc = inc(last, "sT")
            EV.wait_ge(sems["sT"], tc)
            ms = inc(EV.tensor_tensor(rh3(Smat[:]), rh3(Smat[:]),
                                      ctd_c[:, :, None].broadcast_to((128, H, HD)),
                                      op=Alu.mult), "sQ")
            EV.wait_ge(sems["sQ"], ms)
            vc = 0
            for grp in (0, 1):
                vc = inc(EV.tensor_tensor(Smat[:, grp * 512:(grp + 1) * 512],
                                          Smat[:, grp * 512:(grp + 1) * 512],
                                          PBK[3 + grp][:, 0:512], op=Alu.add), "sV")
            bank_guard[3] = ("sV", vc)
            bank_guard[4] = ("sV", vc)

        for _ in p3a:
            pass
        def afb(c0, n):
            return AF[:, c0:c0 + n].bitcast(BF16)

        EY.wait_ge(sems["sT"], cnt["sT"])
        EY.wait_ge(sems["sSP"], cnt["sSP"])
        EY.wait_ge(sems["sP0"], cnt["sP0"])
        EY.wait_ge(sems["sP1"], cnt["sP1"])
        inc(EY.dma_start(WT_h[:, :], swt[0][:, :]), "sLD", 16)
        inc(EY.dma_start(WT_l[:, :], swt[1][:, :]), "sLD", 16)
        stage_free = [0, 0]
        fm_ready = [0, 0]
        g_guard = [0, 0]
        ot_free = [0, 0]
        ld3_mark = [0, 0]
        xt_guard = [0]
        for hp in range(2):
            EY.wait_ge(sems["sT"], cnt["sT"])
            EY.wait_ge(sems["sSP"], cnt["sSP"])
            EY.wait_ge(sems["sP0"], cnt["sP0"])
            EY.wait_ge(sems["sP1"], cnt["sP1"])
            g_guard = [0, 0]
            for ic in range(NIC):
                inc(EY.dma_start_transpose(
                    AB[:, bWD_H + ic * 512:bWD_H + ic * 512 + 512],
                    swd[0][hp * 512:(hp + 1) * 512, ic * 128:(ic + 1) * 128]), "sLD", 16)
                inc(EY.dma_start_transpose(
                    AB[:, bWD_L + ic * 512:bWD_L + ic * 512 + 512],
                    swd[1][hp * 512:(hp + 1) * 512, ic * 128:(ic + 1) * 128]), "sLD", 16)
            for tile in range(NT):
                par = tile % 2
                psem2p = "sP0" if par == 0 else "sP1"
                if hp == 1:
                    pGHp = fGH if par == 0 else fGH2
                    pOTp = fOT if par == 0 else fOT2
                    if g_guard[par]:
                        EY.wait_ge(sems[psem2p], g_guard[par])
                    if ot_free[par]:
                        EY.wait_ge(sems["sT"], ot_free[par])
                    if xt_guard[0]:
                        EY.wait_ge(sems["sV"], xt_guard[0])
                    inc(EY.dma_start(AF[:, pGHp + 512:pGHp + 1024],
                                     sg_d[tile * 128:(tile + 1) * 128, :]), "sLD3", 16)
                    inc(EY.dma_start(af(pOTp), so_d[tile * 128:(tile + 1) * 128, :]),
                        "sLD3", 16)
                    inc(EY.dma_start(af(fXT), xo_d[tile * 128:(tile + 1) * 128, :]),
                        "sLD3", 16)
                    ld3_mark[par] = cnt["sLD3"]
                EY.wait_ge(sems["sT"], stage_free[par])
                hsem = "sLD0" if par == 0 else "sLD1"
                inc(EY.dma_start(afb(fSTG[par][0], 1408), shq[0][tile * 128:(tile + 1) * 128, :]),
                    hsem, 16)
                inc(EY.dma_start(afb(fSTG[par][1], 1408), shq[1][tile * 128:(tile + 1) * 128, :]),
                    hsem, 16)
                hq_lc = cnt[hsem]
                ET.wait_ge(sems[hsem], hq_lc)
                src_list = [(fSTG[par][pc], fFM[par][pc]) for pc in range(2)]
                blocks = [(pc, ic) for pc in range(2) for ic in range(NIC)]
                bi = 0
                while bi < len(blocks):
                    nb = min(8, len(blocks) - bi)
                    use_bank(0)
                    last = None
                    for j in range(nb):
                        pc, ic = blocks[bi + j]
                        last = ET.transpose(
                            PBK[0][:, j * 64:j * 64 + 64].bitcast(BF16),
                            afb(fSTG[par][pc], 1408)[:, ic * 128:(ic + 1) * 128],
                            idnb[:])
                    tc = inc(last, "sT")
                    EV.wait_ge(sems["sT"], tc)
                    vc = 0
                    for j in range(nb):
                        pc, ic = blocks[bi + j]
                        vc = inc(EV.tensor_copy(
                            afb(fFM[par][pc], 1408)[:, ic * 128:(ic + 1) * 128],
                            PBK[0][:, j * 64:j * 64 + 64].bitcast(BF16)), "sV")
                    bank_guard[0] = ("sV", vc)
                    bi += nb
                stage_free[par] = cnt["sT"]
                fm_ready[par] = cnt["sV"]
                w(ET, "sLD")
                w(ET, "sV", fm_ready[par])
                psb = 1 + par
                use_bank(psb)
                lh = [afb(fFM[par][0], 1408)[:, ic * 128:(ic + 1) * 128] for ic in range(NIC)]
                ll = [afb(fFM[par][1], 1408)[:, ic * 128:(ic + 1) * 128] for ic in range(NIC)]
                rh = [AB[:, bWD_H + ic * 512:bWD_H + (ic + 1) * 512] for ic in range(NIC)]
                rl = [AB[:, bWD_L + ic * 512:bWD_L + (ic + 1) * 512] for ic in range(NIC)]
                tc = mm3(psb, 512, lh, ll, rh, rl)
                ES.wait_ge(sems["sT"], tc)
                pXN = fXN if par == 0 else fXN2
                pST = fST if par == 0 else fST2
                pGH = fGH if par == 0 else fGH2
                pOT = fOT if par == 0 else fOT2
                pOF = bOFH if par == 0 else bOF2
                psem2 = "sP0" if par == 0 else "sP1"
                if hp == 0:
                    if g_guard[par]:
                        w(ES, psem2, g_guard[par])
                    sc = inc(ES.activation(af(pGH, 512), PBK[psb][:, 0:512],
                                           Act.Silu), "sS")
                    bank_guard[psb] = ("sS", sc)
                    w(EG, "sS", sc)
                    inc(EG.dma_start(sg_d[tile * 128:(tile + 1) * 128, :], af(pGH, 512)),
                        psem2, 16)
                    g_guard[par] = cnt[psem2]
                else:
                    if g_guard[par]:
                        w(ES, psem2, g_guard[par])
                    sc = inc(ES.activation(af(pGH, 512), PBK[psb][:, 0:512],
                                           Act.Silu), "sS")
                    bank_guard[psb] = ("sS", sc)
                    w(EV, "sLD3", ld3_mark[par])
                    ln_ok = layernorm(fXT, pXN, sq_off=fSQ3)
                    ot_free_pending = True
                    xt_guard[0] = cnt["sV"]
                    w(ET, "sV")
                    transpose_to(af(pOT), 8, 0, evac_split(AB[:, pOF:pOF + D],
                                                           AB[:, pOF + 1024:pOF + 1024 + D]
                                                           if False else
                                                           AB[:, pOF + 1024:pOF + 2048], 0))
                    of_ok = cnt["sV"]
                    ot_free[par] = cnt["sT"]
                    lh2 = [AB[:, pOF + kc * 128:pOF + (kc + 1) * 128] for kc in range(8)]
                    ll2 = [AB[:, pOF + 1024 + kc * 128:pOF + 1024 + (kc + 1) * 128]
                           for kc in range(8)]
                    w(ET, "sV", of_ok)
                    w(ET, "sLD")
                    rtc = []
                    for pan2 in (0, 1):
                        use_bank(5 + pan2)
                        rh2 = [WT_h[:, kc * D + pan2 * 512:kc * D + pan2 * 512 + 512]
                               for kc in range(8)]
                        rl2 = [WT_l[:, kc * D + pan2 * 512:kc * D + pan2 * 512 + 512]
                               for kc in range(8)]
                        rtc.append(mm3(5 + pan2, 512, lh2, ll2, rh2, rl2))
                    EV.wait_ge(sems["sT"], rtc[-1])
                    EV.wait_ge(sems["sV"], ln_ok)
                    vc = 0
                    for pan2 in (0, 1):
                        vc = inc(EV.tensor_tensor(
                            AF[:, pST + pan2 * 512:pST + (pan2 + 1) * 512],
                            af(pXN)[:, pan2 * 512:(pan2 + 1) * 512],
                            PBK[5 + pan2][:, 0:512], op=Alu.add), "sV")
                    bank_guard[5] = ("sV", vc)
                    bank_guard[6] = ("sV", vc)
                    w(EV, "sS", sc)
                    EV.wait_ge(sems["sV"], vc)
                    mg = 0
                    mg = inc(EV.tensor_tensor(AF[:, pST:pST + 512], AF[:, pST:pST + 512],
                                              AF[:, pGH + 512:pGH + 1024],
                                              op=Alu.mult), "sQ")
                    mg = inc(EV.tensor_tensor(AF[:, pST + 512:pST + 1024],
                                              AF[:, pST + 512:pST + 1024],
                                              AF[:, pGH:pGH + 512], op=Alu.mult), "sQ")
                    EV.wait_ge(sems["sQ"], mg)
                    last = EV.tensor_tensor(AF[:, pST:pST + 1024], AF[:, pST:pST + 1024],
                                            af(fXT), op=Alu.add)
                    vc = inc(last, "sV")
                    xt_guard[0] = vc
                    w(EG, "sV", vc)
                    inc(EG.dma_start(y_d[tile * 128:(tile + 1) * 128, :],
                                     AF[:, pST:pST + 1024]), psem2, 16)
                    g_guard[par] = cnt[psem2]

        EG.wait_ge(sems["sSP"], cnt["sSP"])
        EG.wait_ge(sems["sP0"], cnt["sP0"])
        EG.wait_ge(sems["sP1"], cnt["sP1"])
    return nc


def kernel(**inputs):
    xs = np.ascontiguousarray(inputs["L0_attn_out"], dtype=np.float32)
    B, T, _ = xs.shape
    t_own = T // 2
    in_maps = []
    for core in range(8):
        b, half = core // 2, core % 2
        own = xs[b, half * t_own:(half + 1) * t_own]
        in_maps.append({
            "xo": np.ascontiguousarray(own),
            "smask": np.full((1, 1), 1.0 if half == 1 else 0.0, np.float32),
            "Ww": np.ascontiguousarray(inputs["L0_attn_W_write"], dtype=np.float32),
            "Wr": np.ascontiguousarray(inputs["L0_attn_W_read"], dtype=np.float32),
            "Wb": np.ascontiguousarray(inputs["L0_attn_W_beta"], dtype=np.float32),
            "dec": np.ascontiguousarray(inputs["L0_attn_decay"], dtype=np.float32).reshape(1, H),
            "wg": np.ascontiguousarray(inputs["L0_ffn_w_gate"], dtype=np.float32),
            "wu": np.ascontiguousarray(inputs["L0_ffn_w_up"], dtype=np.float32),
            "wd": np.ascontiguousarray(inputs["L0_ffn_w_down"], dtype=np.float32),
        })
    nc = build_nc(t_own=t_own, n_cores=8)
    res = run_bass_kernel_spmd(nc, in_maps, core_ids=list(range(8)))
    out = np.zeros((B, T, D), np.float32)
    for core in range(8):
        b, half = core // 2, core % 2
        out[b, half * t_own:(half + 1) * t_own] = res.results[core]["y"]
    return out


# version 22
# speedup vs baseline: 1.0030x; 1.0030x over previous
import numpy as np
import ml_dtypes
import concourse.bass as bass
import concourse.mybir as mybir
from concourse.bass_utils import run_bass_kernel_spmd
from contextlib import ExitStack

F32 = mybir.dt.float32
BF16 = mybir.dt.bfloat16
I32 = mybir.dt.int32
Act = mybir.ActivationFunctionType
Alu = mybir.AluOpType
AX = mybir.AxisListType

D, H, HD, C, IF = 1024, 8, 128, 128, 2816
NKC = D // 128
NIC = IF // 128
MAGIC = 12582912.0
VELT = float(2**22 + 1)
DBG = False
PANW = [512, 512, 512, 512, 512, 256]
PANO = [0, 512, 1024, 1536, 2048, 2560]

qU, qC1, qT1, qXQ = 0, 1024, 2048, 3072
wLD = [4096, 5120]
qM, qP, qR = 6144, 6400, 6656
mXT, mXN, mKT, mKBT, mOT = 7168, 8192, 9216, 10240, 11264
mKFM, mKBFM, mKDEF = 12288, 13312, 14336
mRR = 15360
mATTT, mKDW, mATT = 17408, 18432, 19456
mPP, mPPT = 20480, 21504
mKCDT, mVNEW = mPPT, mPP
mSPILL, mSPILLW = 14336, 5120
fHT = 4096
fXT, fXN = 6144, 7168
fST, fGH, fOT = 8192, 9216, 10240
fSQ3 = 16896
fXN2, fST2, fGH2, fOT2 = 17920, 18944, 19968, 20992
bOF2 = 22528

bTH, bLO, bM, bSEL, bQH, bQL = 0, 1024, 2048, 3072, 4096, 5120
bXQH, bXQL = 6144, 10240
bWP = [14336, 22528]
bXFH, bXFL = 6144, 7168
bWD_H, bWD_L = 0, 11264
bOFH, bOFL = 28160, 29184
fSTG = [[0, 1408], [2816, 4224]]
fFM = [[11264, 12672], [14080, 15488]]


def build_nc(t_own=2048, t_pre=2048, n_cores=8):
    nc = bass.Bass("TRN2", target_bir_lowering=False)
    NT = t_own // 128
    NCH = t_own // C
    NCHP = t_pre // C

    xo_d = nc.dram_tensor("xo", [t_own, D], F32, kind="ExternalInput").ap()
    sm_d = nc.dram_tensor("smask", [1, 1], F32, kind="ExternalInput").ap()
    Ww_d = nc.dram_tensor("Ww", [D, D], F32, kind="ExternalInput").ap()
    Wr_d = nc.dram_tensor("Wr", [D, D], F32, kind="ExternalInput").ap()
    Wb_d = nc.dram_tensor("Wb", [H, D], F32, kind="ExternalInput").ap()
    dec_d = nc.dram_tensor("dec", [1, H], F32, kind="ExternalInput").ap()
    wg_d = nc.dram_tensor("wg", [IF, D], F32, kind="ExternalInput").ap()
    wu_d = nc.dram_tensor("wu", [IF, D], F32, kind="ExternalInput").ap()
    wd_d = nc.dram_tensor("wd", [D, IF], F32, kind="ExternalInput").ap()
    y_d = nc.dram_tensor("y", [t_own, D], F32, kind="ExternalOutput").ap()
    swg = [nc.dram_tensor(f"swg{p}", [IF, D], BF16, kind="Internal").ap() for p in range(2)]
    swu = [nc.dram_tensor(f"swu{p}", [IF, D], BF16, kind="Internal").ap() for p in range(2)]
    swd = [nc.dram_tensor(f"swd{p}", [D, IF], BF16, kind="Internal").ap() for p in range(2)]
    swt = [nc.dram_tensor(f"swt{p}", [128, NKC * D], BF16, kind="Internal").ap() for p in range(2)]
    shq = [nc.dram_tensor(f"shq{p}", [t_own, IF], BF16, kind="Internal").ap() for p in range(2)]
    so_d = nc.dram_tensor("so", [t_own, D], F32, kind="Internal").ap()
    sck_d = nc.dram_tensor("sck", [t_own, 5120], F32, kind="Internal").ap()
    ccin_t = nc.dram_tensor("ccin", [128, H * 128], F32)
    ccout_t = nc.dram_tensor("ccout", [128, H * 128], F32)
    dbg = {}
    if DBG:
        for nm, wdt in [("xn", D), ("kt", D), ("R0", 2048), ("A0", 1024),
                        ("RN", 2048), ("att", 1024), ("vnew", 1024), ("S1", 1024),
                        ("S2", 1024), ("bet", H), ("kdw", D), ("kdef", D),
                        ("dwc", H), ("dec2", H), ("ctdc", H), ("Lmt", 1024),
                        ("nLst", 1024), ("mtit", 128)]:
            dbg[nm] = nc.dram_tensor("dbg_" + nm, [128, wdt], F32, kind="Internal").ap()
    sg_d = nc.dram_tensor("sgh", [t_own, 512], F32, kind="Internal").ap()

    with ExitStack() as S_:
        sb = lambda n, s, d: S_.enter_context(nc.sbuf_tensor(n, s, d))
        idn = sb("idn", [128, 128], F32)
        idnb = sb("idnb", [128, 128], BF16)
        iot = sb("iot", [128, 128], I32)
        irow = sb("irow", [128, 128], I32)
        idf = sb("idf", [128, 128], F32)
        irf = sb("irf", [128, 128], F32)
        mti = sb("mti", [128, 128], F32)
        mts = sb("mts", [128, 128], F32)
        lgb = sb("lgb", [128, H], F32)
        smb = sb("smb", [128, 1], F32)
        ismb = sb("ismb", [128, 1], F32)
        de_c = sb("de_c", [128, H], F32)
        dw_c = sb("dw_c", [128, H], F32)
        ctd_c = sb("ctd_c", [128, H], F32)
        de_f = sb("de_f", [128, H * 128], F32)
        Lm = sb("Lm", [128, H * 128], F32)
        nLs = sb("nLs", [128, H * 128], F32)
        Smat = sb("Smat", [128, H * 128], F32)
        colv = sb("colv", [128, 1], F32)
        rn = sb("rn", [128, H], F32)
        bet = sb("bet", [128, H], F32)
        lnm = sb("lnm", [128, 1], F32)
        lnv = sb("lnv", [128, 1], F32)
        WT_h = sb("WT_h", [128, NKC * D], BF16)
        WT_l = sb("WT_l", [128, NKC * D], BF16)
        WbT_h = sb("WbT_h", [128, NKC * H], BF16)
        WbT_l = sb("WbT_l", [128, NKC * H], BF16)
        AF = sb("AF", [128, 22528], F32)
        AB = sb("AB", [128, 30720], BF16)
        PBK = [S_.enter_context(nc.psum_tensor(f"PB{i}", [128, 512], F32)) for i in range(8)]

        sems, cnt = {}, {}
        for s in ["sLD", "sLD0", "sLD1", "sLD3", "sIN", "sSP", "sP0", "sP1", "sCC", "sV", "sS", "sT", "sG",
                  "sV2", "sS2", "sT2", "sG2", "sQ", "sQS"]:
            sems[s] = S_.enter_context(nc.semaphore(s))
            cnt[s] = 0

        blk = S_.enter_context(nc.Block())
        EV, ES, ET, EG, EY = nc.vector, nc.scalar, nc.tensor, nc.gpsimd, nc.sync

        R = {"on": False}

        def rs(s):
            return s + "2" if (R["on"] and s in ("sT", "sV", "sS", "sG")) else s

        def inc(inst, s, k=1):
            s = rs(s)
            inst.then_inc(sems[s], k)
            cnt[s] += k
            return cnt[s]

        def w(eng, s, val=None):
            s = rs(s)
            eng.wait_ge(sems[s], cnt[s] if val is None else val)

        def vmark():
            EV.drain()
            return inc(EV.nop(), "sV")

        def smark():
            ES.drain()
            return inc(ES.nop(), "sS")

        bank_guard = {}

        def use_bank(i):
            g = bank_guard.pop(i, None)
            if g is not None:
                ET.wait_ge(sems[g[0]], g[1])

        def af(off, n=1024):
            return AF[:, off:off + n]

        def ab(off, n=1024):
            return AB[:, off:off + n]

        def r3(apx, k=16):
            return apx.rearrange("p (b k) -> p b k", k=k)

        def rh3(apx):
            return apx.rearrange("p (h d) -> p h d", h=H)

        def dump(nm, src_ap):
            if not DBG:
                return
            w(EG, "sV")
            w(EG, "sS")
            inc(EG.dma_start(dbg[nm][:, :], src_ap), "sSP", 16)
            w(EG, "sSP")
            w(EV, "sSP")
            w(ES, "sSP")

        qlast = [0]

        def quant(src, F, split=True, qh_off=None, ql_off=None):
            NB = F // 16
            qh_off = bQH if qh_off is None else qh_off
            ql_off = bQL if ql_off is None else ql_off
            pm, pp, pr = AF[:, qM:qM + NB], AF[:, qP:qP + NB], AF[:, qR:qR + NB]
            if qlast[0]:
                EV.wait_ge(sems[qlast[0][0]], qlast[0][1])
            F2, NB2 = F // 2, NB // 2
            sh = [src[:, 0:F2], src[:, F2:F]]

            def hv(off, h):
                return AF[:, off + h * F2:off + (h + 1) * F2]

            def hb(off, h):
                return AB[:, off + h * F2:off + (h + 1) * F2]

            pmh = [pm[:, 0:NB2], pm[:, NB2:NB]]
            pph = [pp[:, 0:NB2], pp[:, NB2:NB]]
            prh = [pr[:, 0:NB2], pr[:, NB2:NB]]
            mk = {}
            for h in (0, 1):
                mk["r", h] = inc(EV.tensor_reduce(
                    pmh[h], r3(sh[h]), axis=AX.X, op=Alu.max,
                    apply_absolute_value=True), "sQ")
            for h in (0, 1):
                EV.wait_ge(sems["sQ"], mk["r", h])
                mk["p", h] = inc(EV.tensor_scalar(
                    pph[h], pmh[h], 1.0 / 12.0, 5e-13,
                    op0=Alu.mult, op1=Alu.max), "sQ")
            for h in (0, 1):
                EV.wait_ge(sems["sQ"], mk["p", h])
                mk["i", h] = inc(EV.reciprocal(prh[h], pph[h]), "sQ")
            u_ok = 0
            for h in (0, 1):
                EV.wait_ge(sems["sQ"], mk["i", h])
                u_ok = inc(EV.tensor_tensor(
                    r3(hv(qU, h)), r3(sh[h]),
                    prh[h][:, :, None].broadcast_to((128, NB2, 16)),
                    op=Alu.mult), "sV")
            w(ES, "sV", u_ok)
            xq_f = inc(ES.activation(af(qXQ, F), af(qU, F), Act.Copy, bias=MAGIC), "sQS")
            ES.activation(af(qC1, F), af(qU, F), Act.Copy, scale=VELT)
            ES.wait_ge(sems["sQS"], xq_f)
            lo_ok = inc(ES.activation(ab(bLO, F), af(qXQ, F), Act.Copy, bias=-MAGIC), "sS")
            w(EV, "sS", lo_ok)
            t1_f = inc(EV.tensor_tensor(af(qT1, F), af(qC1, F), af(qU, F),
                                        op=Alu.subtract), "sQ")
            ab_f = inc(EV.tensor_scalar(af(qXQ, F).bitcast(I32), af(qU, F).bitcast(I32),
                                        0x7FFFFFFF, None, op0=Alu.bitwise_and), "sQ")
            EV.wait_ge(sems["sQ"], t1_f)
            th_f = inc(EV.tensor_tensor(ab(bTH, F), af(qC1, F), af(qT1, F),
                                        op=Alu.subtract), "sQ")
            EV.wait_ge(sems["sQ"], ab_f)
            m_f = inc(EV.tensor_scalar(ab(bM, F), af(qXQ, F), 1.75, None,
                                       op0=Alu.is_gt), "sQ")
            EV.wait_ge(sems["sQ"], th_f)
            s1 = {}
            for h in (0, 1):
                s1[h] = inc(EV.tensor_tensor(hb(bSEL, h), hb(bTH, h), hb(bLO, h),
                                             op=Alu.subtract), "sQ")
            EV.wait_ge(sems["sQ"], m_f)
            s2 = {}
            for h in (0, 1):
                EV.wait_ge(sems["sQ"], s1[h])
                s2[h] = inc(EV.tensor_tensor(hb(bSEL, h), hb(bM, h), hb(bSEL, h),
                                             op=Alu.mult), "sQ")
            sel_ok = 0
            for h in (0, 1):
                EV.wait_ge(sems["sQ"], s2[h])
                sel_ok = inc(EV.tensor_tensor(hb(bSEL, h), hb(bLO, h), hb(bSEL, h),
                                              op=Alu.add), "sV")
            w(EG, "sV", sel_ok)
            xq_ok = inc(EG.tensor_tensor(r3(af(qXQ, F)), r3(ab(bSEL, F)),
                        pp[:, :, None].broadcast_to((128, NB, 16)), op=Alu.mult), "sG")
            if split:
                w(ES, "sG", xq_ok)
                qh_ok = inc(ES.activation(ab(qh_off, F), af(qXQ, F), Act.Copy), "sS")
                w(EG, "sS", qh_ok)
                xq_ok = inc(EG.tensor_tensor(ab(ql_off, F), af(qXQ, F), ab(qh_off, F),
                                             op=Alu.subtract), "sG")
            qlast[0] = (rs("sG"), xq_ok)
            return xq_ok

        def layernorm(src_off, dst_off, sq_off=None):
            sq_off = src_off if sq_off is None else sq_off
            q1 = inc(EV.tensor_reduce(lnm[:], af(src_off), axis=AX.X, op=Alu.add), "sQ")
            EV.wait_ge(sems["sQ"], q1)
            q2 = inc(EV.tensor_scalar(lnm[:], lnm[:], 1.0 / D, None, op0=Alu.mult), "sQ")
            EV.wait_ge(sems["sQ"], q2)
            v1 = inc(EV.tensor_scalar(af(dst_off), af(src_off), lnm[:], None,
                                      op0=Alu.subtract), "sV")
            w(ES, "sV", v1)
            s1 = inc(ES.activation(af(sq_off), af(dst_off), Act.Square), "sS")
            w(EV, "sS", s1)
            q3 = inc(EV.tensor_reduce(lnv[:], af(sq_off), axis=AX.X, op=Alu.add), "sQ")
            EV.wait_ge(sems["sQ"], q3)
            v2 = inc(EV.tensor_scalar(lnv[:], lnv[:], 1.0 / D, 1e-5,
                                      op0=Alu.mult, op1=Alu.add), "sV")
            w(ES, "sV", v2)
            s2 = inc(ES.activation(lnv[:], lnv[:], Act.Sqrt), "sS")
            w(EV, "sS", s2)
            mr = inc(EV.reciprocal(lnv[:], lnv[:]), "sQ")
            EV.wait_ge(sems["sQ"], mr)
            return inc(EV.tensor_scalar(af(dst_off), af(dst_off), lnv[:], None,
                                        op0=Alu.mult), "sV")

        def tr_blocks(src, nblk, bank):
            batches = []
            done = 0
            while done < nblk:
                nb = min(4, nblk - done)
                batches.append((done, nb))
                done += nb
            return batches

        def transpose_to(src, nblk, bank, evac):
            for (j0, nb) in tr_blocks(src, nblk, bank):
                use_bank(bank)
                last = None
                for j in range(nb):
                    last = ET.transpose(PBK[bank][:, j * 128:(j + 1) * 128],
                                        src[:, (j0 + j) * 128:(j0 + j + 1) * 128], idn[:])
                tc = inc(last, "sT")
                evac(j0, nb, tc)

        def evac_split(dsth, dstl, bank):
            def ev(j0, nb, tc):
                ES.wait_ge(sems["sT"], tc)
                sc = inc(ES.activation(dsth[:, j0 * 128:(j0 + nb) * 128],
                                       PBK[bank][:, 0:nb * 128], Act.Copy), "sS")
                EV.wait_ge(sems["sS"], sc)
                vc = inc(EV.tensor_tensor(dstl[:, j0 * 128:(j0 + nb) * 128],
                                          PBK[bank][:, 0:nb * 128],
                                          dsth[:, j0 * 128:(j0 + nb) * 128],
                                          op=Alu.subtract), "sV")
                bank_guard[bank] = (rs("sV"), vc)
            return ev

        def evac_copy(dst, bank):
            def ev(j0, nb, tc):
                EV.wait_ge(sems["sT"], tc)
                vc = inc(EV.tensor_copy(dst[:, j0 * 128:(j0 + nb) * 128],
                                        PBK[bank][:, 0:nb * 128]), "sV")
                bank_guard[bank] = (rs("sV"), vc)
            return ev

        def mm3(bank, ncol, lh, ll, rh, rl):
            last = None
            n = len(lh)
            for i in range(n):
                last = ET.matmul(PBK[bank][:, 0:ncol], lh[i], rh[i],
                                 start=(i == 0), stop=False)
                last = ET.matmul(PBK[bank][:, 0:ncol], lh[i], rl[i],
                                 start=False, stop=False)
                last = ET.matmul(PBK[bank][:, 0:ncol], ll[i], rh[i],
                                 start=False, stop=(i == n - 1))
            return inc(last, "sT")

        inc(EY.dma_start(lgb[:], dec_d[0:1, :].partition_broadcast(128)), "sIN", 16)
        inc(EY.dma_start(smb[:], sm_d[0:1, :].partition_broadcast(128)), "sIN", 16)
        EG.iota(iot[:], [[-1, 128]], base=0, channel_multiplier=1)
        EG.iota(irow[:], [[1, 128]], base=1, channel_multiplier=0)
        EG.drain()
        EG.tensor_scalar(idn[:], iot[:], 0.0, None, op0=Alu.is_equal)
        EG.drain()
        EG.tensor_copy(idnb[:], idn[:])
        EG.tensor_copy(idf[:], iot[:])
        EG.tensor_copy(irf[:], irow[:])
        EG.drain()
        g0 = inc(EG.nop(), "sV")
        w(EV, "sIN")
        EV.tensor_scalar(ismb[:], smb[:], -1.0, 1.0, op0=Alu.mult, op1=Alu.add)
        EV.drain()
        w(ES, "sIN")
        ES.activation(lgb[:], lgb[:], Act.Sigmoid)
        ES.drain()
        ES.activation(lgb[:], lgb[:], Act.Ln)
        lg_ok = smark()
        w(EV, "sV", g0)
        EV.tensor_scalar(mti[:], idf[:], -0.5, None, op0=Alu.is_gt)
        EV.tensor_scalar(mts[:], idf[:], 0.5, None, op0=Alu.is_gt)
        w(EV, "sS", lg_ok)
        EV.drain()
        EV.tensor_tensor(rh3(Lm[:]), idf[:, None, :].broadcast_to((128, H, 128)),
                         lgb[:, :, None].broadcast_to((128, H, 128)), op=Alu.mult)
        EV.drain()
        EV.tensor_tensor(rh3(de_f[:]), irf[:, None, :].broadcast_to((128, H, 128)),
                         lgb[:, :, None].broadcast_to((128, H, 128)), op=Alu.mult)
        v1 = vmark()
        w(ES, "sV", v1)
        ES.activation(Lm[:], Lm[:], Act.Exp)
        ES.activation(de_f[:], de_f[:], Act.Exp)
        s1 = smark()
        w(EV, "sS", s1)
        EV.tensor_tensor(rh3(Lm[:]), rh3(Lm[:]),
                         mti[:, None, :].broadcast_to((128, H, 128)), op=Alu.mult)
        EV.drain()
        EV.tensor_tensor(rh3(nLs[:]), rh3(Lm[:]),
                         mts[:, None, :].broadcast_to((128, H, 128)), op=Alu.mult)
        EV.drain()
        EV.tensor_scalar(nLs[:], nLs[:], -1.0, None, op0=Alu.mult)
        EV.tensor_scalar(colv[:], idf[:, 0:1], 1.0, None, op0=Alu.add)
        EV.drain()
        EV.tensor_tensor(de_c[:], colv[:].broadcast_to((128, H)), lgb[:, 0:H], op=Alu.mult)
        EV.drain()
        EV.tensor_scalar(colv[:], idf[:, 0:1], -1.0, float(C - 1), op0=Alu.mult, op1=Alu.add)
        EV.drain()
        EV.tensor_tensor(dw_c[:], colv[:].broadcast_to((128, H)), lgb[:, 0:H], op=Alu.mult)
        EV.drain()
        EV.memset(colv[:], float(C))
        EV.drain()
        EV.tensor_tensor(ctd_c[:], colv[:].broadcast_to((128, H)), lgb[:, 0:H], op=Alu.mult)
        v2 = vmark()
        w(ES, "sV", v2)
        ES.activation(de_c[:], de_c[:], Act.Exp)
        ES.activation(dw_c[:], dw_c[:], Act.Exp)
        ES.activation(ctd_c[:], ctd_c[:], Act.Exp)
        tables_ok_s = smark()
        EV.memset(Smat[:], 0.0)
        smat0 = vmark()
        if DBG:
            dump("dwc", dw_c[:])
            dump("dec2", de_c[:])
            dump("ctdc", ctd_c[:])
            dump("Lmt", Lm[:])
            dump("nLst", nLs[:])
            dump("mtit", mti[:])

        wq_slices = []
        for (src, dh, dl, rows, colsF) in [
            (wg_d, swg[0], swg[1], IF, D),
            (wu_d, swu[0], swu[1], IF, D),
            (wd_d, swd[0], swd[1], D, IF),
        ]:
            for r0 in range(0, rows, 128):
                for f0 in range(0, colsF, 1024):
                    wq_slices.append((src, dh, dl, r0, f0, min(1024, colsF - f0)))
        wq_state = {"j": 0, "buf_free": [0, 0], "spill": [0, 0], "t0": None}

        def pump_wq(nj=1):
            for _ in range(nj):
                j = wq_state["j"]
                if j >= len(wq_slices):
                    return
                if wq_state["t0"] is None:
                    wq_state["t0"] = cnt["sT"]
                wq_state["j"] = j + 1
                (src, dh, dl, r0, f0, F) = wq_slices[j]
                par = j % 2
                lsem = "sLD0" if par == 0 else "sLD1"
                psem = "sP0" if par == 0 else "sP1"
                qh_o = 4096 if par == 0 else 8192
                ql_o = 5120 if par == 0 else 9216
                if wq_state["buf_free"][par]:
                    EY.wait_ge(sems["sV"], wq_state["buf_free"][par])
                else:
                    EY.wait_ge(sems["sT"], wq_state["t0"])
                inc(EY.dma_start(af(wLD[par], F), src[r0:r0 + 128, f0:f0 + F]), lsem, 16)
                w(EV, lsem)
                if wq_state["spill"][par]:
                    w(ES, psem, wq_state["spill"][par])
                    w(EG, psem, wq_state["spill"][par])
                quant(af(wLD[par], F), F, split=True, qh_off=qh_o, ql_off=ql_o)
                w(EG, "sG")
                inc(EG.dma_start(dh[r0:r0 + 128, f0:f0 + F], ab(qh_o, F)), psem, 16)
                inc(EG.dma_start(dl[r0:r0 + 128, f0:f0 + F], ab(ql_o, F)), psem, 16)
                wq_state["spill"][par] = cnt[psem]
                wq_state["buf_free"][par] = cnt["sV"]

        for (Wsrc, spill) in [(Wr_d, True), (Ww_d, False)]:
            for r0 in range(0, D, 128):
                EY.wait_ge(sems["sT"], cnt["sT"])
                inc(EY.dma_start(af(wLD[0], D), Wsrc[r0:r0 + 128, :]), "sLD", 16)
                w(ET, "sLD")

                def mk(dh_, dl_, r0_):
                    def ev(j0, nb, tc):
                        ES.wait_ge(sems["sT"], tc)
                        sc = 0
                        for jj in range(nb):
                            kc = j0 + jj
                            sc = inc(ES.activation(dh_[:, kc * D + r0_:kc * D + r0_ + 128],
                                                   PBK[3][:, jj * 128:(jj + 1) * 128],
                                                   Act.Copy), "sS")
                        EV.wait_ge(sems["sS"], sc)
                        vc = 0
                        for jj in range(nb):
                            kc = j0 + jj
                            vc = inc(EV.tensor_tensor(
                                dl_[:, kc * D + r0_:kc * D + r0_ + 128],
                                PBK[3][:, jj * 128:(jj + 1) * 128],
                                dh_[:, kc * D + r0_:kc * D + r0_ + 128],
                                op=Alu.subtract), "sV")
                        bank_guard[3] = ("sV", vc)
                    return ev
                transpose_to(af(wLD[0], D), 8, 3, mk(WT_h, WT_l, r0))
            if spill:
                w(EG, "sV")
                inc(EG.dma_start(swt[0][:, :], WT_h[:, :]), "sSP", 16)
                inc(EG.dma_start(swt[1][:, :], WT_l[:, :]), "sSP", 16)
                w(ES, "sSP")
                w(EV, "sSP")
        EY.wait_ge(sems["sT"], cnt["sT"])
        inc(EY.dma_start(AF[0:H, wLD[0]:wLD[0] + D], Wb_d[:, :]), "sLD", 16)
        w(ET, "sLD")
        use_bank(3)
        last = None
        for kc in range(8):
            last = ET.transpose(PBK[3][:, kc * 8:kc * 8 + 8],
                                AF[0:H, wLD[0] + kc * 128:wLD[0] + (kc + 1) * 128],
                                idn[0:H, 0:H])
        tc = inc(last, "sT")
        ES.wait_ge(sems["sT"], tc)
        ES.activation(WbT_h[:, :], PBK[3][:, 0:H * NKC], Act.Copy)
        sc = smark()
        EV.wait_ge(sems["sS"], sc)
        vc = inc(EV.tensor_tensor(WbT_l[:, :], PBK[3][:, 0:H * NKC], WbT_h[:, :],
                                  op=Alu.subtract), "sV")
        bank_guard[3] = ("sV", vc)

        mx_guard = [0]
        ck_guard = [0]
        fl_state = {"k_ok": 0}

        def front_lite(x_dram, row0):
            if mx_guard[0]:
                EY.wait_ge(sems["sV"], mx_guard[0])
            inc(EY.dma_start(af(mXT), x_dram[row0:row0 + 128, :]), "sLD3", 16)
            yield
            w(EV, "sLD3")
            m1 = inc(EV.tensor_reduce(lnm[:], af(mXT), axis=AX.X, op=Alu.add), "sQ")
            EV.wait_ge(sems["sQ"], m1)
            m2 = inc(EV.tensor_scalar(lnm[:], lnm[:], 1.0 / D, None, op0=Alu.mult), "sQ")
            EV.wait_ge(sems["sQ"], m2)
            m3 = inc(EV.tensor_scalar(af(mXN), af(mXT), lnm[:], None,
                                      op0=Alu.subtract), "sQ")
            yield
            EV.wait_ge(sems["sQ"], m3)
            m4 = inc(EV.tensor_tensor(af(mXT), af(mXN), af(mXN), op=Alu.mult), "sQ")
            EV.wait_ge(sems["sQ"], m4)
            m5 = inc(EV.tensor_reduce(lnv[:], af(mXT), axis=AX.X, op=Alu.add), "sQ")
            EV.wait_ge(sems["sQ"], m5)
            v2 = inc(EV.tensor_scalar(lnv[:], lnv[:], 1.0 / D, 1e-5,
                                      op0=Alu.mult, op1=Alu.add), "sV")
            yield
            w(ES, "sV", v2)
            s2 = inc(ES.activation(lnv[:], lnv[:], Act.Sqrt), "sS")
            w(EV, "sS", s2)
            m6 = inc(EV.reciprocal(lnv[:], lnv[:]), "sQ")
            EV.wait_ge(sems["sQ"], m6)
            m7 = inc(EV.tensor_scalar(af(mXN), af(mXN), lnv[:], None,
                                      op0=Alu.mult), "sV")
            yield
            w(EV, "sV", m7)
            m8 = inc(EV.tensor_tensor(af(mXT), af(mXN), af(mXN), op=Alu.mult), "sQ")
            EV.wait_ge(sems["sQ"], m8)
            v1 = inc(EV.tensor_reduce(rn[:], rh3(af(mXT)), axis=AX.X, op=Alu.add), "sV")
            mx_guard[0] = v1
            yield
            w(ES, "sV", v1)
            s3 = inc(ES.activation(rn[:], rn[:], Act.Sqrt), "sS")
            w(EV, "sS", s3)
            m9 = inc(EV.tensor_scalar(rn[:], rn[:], 1e-12, None, op0=Alu.max), "sQ")
            EV.wait_ge(sems["sQ"], m9)
            m10 = inc(EV.reciprocal(rn[:], rn[:]), "sQ")
            yield
            EV.wait_ge(sems["sQ"], m10)
            fl_state["k_ok"] = inc(EV.tensor_tensor(
                rh3(af(mKT)), rh3(af(mXN)),
                rn[:, :, None].broadcast_to((128, H, HD)), op=Alu.mult), "sV")

        def mixer_chunk(row0, with_o, o_dram, fl_next=None, spill=False, dmp=False):
            if spill and ck_guard[0]:
                w(EV, "sSP", ck_guard[0])
            k_ok = fl_state["k_ok"]
            w(ET, "sV")
            transpose_to(af(mXN), 8, 0, evac_split(AB[:, bXFH:bXFH + D],
                                                   AB[:, bXFL:bXFL + D], 0))
            if dmp:
                dump("xn", af(mXN))
                dump("kt", af(mKT))

            lh = [AB[:, bXFH + kc * 128:bXFH + (kc + 1) * 128] for kc in range(8)]
            ll = [AB[:, bXFL + kc * 128:bXFL + (kc + 1) * 128] for kc in range(8)]
            w(ET, "sV")
            vtc = []
            for pan in (0, 1):
                use_bank(1 + pan)
                rh = [WT_h[:, kc * D + pan * 512:kc * D + pan * 512 + 512] for kc in range(8)]
                rl = [WT_l[:, kc * D + pan * 512:kc * D + pan * 512 + 512] for kc in range(8)]
                vtc.append(mm3(1 + pan, 512, lh, ll, rh, rl))
            use_bank(7)
            rbh = [WbT_h[:, kc * H:(kc + 1) * H] for kc in range(8)]
            rbl = [WbT_l[:, kc * H:(kc + 1) * H] for kc in range(8)]
            btc = mm3(7, H, lh, ll, rbh, rbl)
            ES.wait_ge(sems["sT"], btc)
            bet_ok = inc(ES.activation(bet[:], PBK[7][:, 0:H], Act.Sigmoid), "sS")
            bank_guard[7] = ("sS", bet_ok)
            EV.wait_ge(sems["sS"], bet_ok)
            rrv = AF[:, mRR:mRR + 2048].rearrange("p (h t) -> p h t", h=H)
            for pan in (0, 1):
                EV.tensor_tensor(rrv[:, pan * 4:(pan + 1) * 4, 0:128],
                                 PBK[1 + pan][:, 0:512].rearrange("p (h d) -> p h d", h=4),
                                 bet[:, pan * 4:(pan + 1) * 4, None].broadcast_to((128, 4, HD)),
                                 op=Alu.mult)
            vkb = inc(EV.tensor_tensor(rh3(af(mKBT)), rh3(af(mKT)),
                                       bet[:, :, None].broadcast_to((128, H, HD)),
                                       op=Alu.mult), "sV")
            bank_guard[1] = ("sV", vkb)
            bank_guard[2] = ("sV", vkb)
            EV.tensor_tensor(rh3(af(mKDW)), rh3(af(mKT)),
                             dw_c[:, :, None].broadcast_to((128, H, HD)), op=Alu.mult)
            EV.wait_ge(sems["sV"], vkb)
            rr_ok = inc(EV.tensor_tensor(rrv[:, :, 128:256], rh3(af(mKBT)),
                                         de_c[:, :, None].broadcast_to((128, H, HD)),
                                         op=Alu.mult), "sV")
            if dmp:
                dump("bet", bet[:])
                dump("R0", AF[:, mRR:mRR + 2048])
                dump("kdw", af(mKDW))

            w(ET, "sV", rr_ok)
            transpose_to(af(mKT), 8, 0, evac_copy(af(mKFM), 0))
            transpose_to(af(mKBT), 8, 0, evac_copy(af(mKBFM), 0))
            w(EV, "sV")
            kdef_ok = inc(EV.tensor_tensor(af(mKDEF), af(mKFM), de_f[:, :],
                                           op=Alu.mult), "sV")

            kf = [af(mKFM)[:, h * 128:(h + 1) * 128] for h in range(H)]
            kbf = [af(mKBFM)[:, h * 128:(h + 1) * 128] for h in range(H)]
            w(ET, "sV", kdef_ok)
            for grp in (0, 1):
                use_bank(3 + grp)
            last = None
            for h in range(H):
                last = ET.matmul(PBK[3 + h // 4][:, (h % 4) * 128:(h % 4) * 128 + 128],
                                 kbf[h], kf[h], start=True, stop=True)
            tc = inc(last, "sT")
            EV.wait_ge(sems["sT"], tc)
            vc = 0
            for grp in (0, 1):
                vc = inc(EV.tensor_tensor(AF[:, mPP + grp * 512:mPP + (grp + 1) * 512],
                                          PBK[3 + grp][:, 0:512],
                                          nLs[:, grp * 512:(grp + 1) * 512],
                                          op=Alu.mult), "sV")
            bank_guard[3] = ("sV", vc)
            bank_guard[4] = ("sV", vc)
            if dmp:
                dump("A0", AF[:, mPP:mPP + 1024])
                dump("kdef", af(mKDEF))
            if True:
                w(ET, "sV", vc)
                for grp in (0, 1):
                    use_bank(3 + grp)
                last = None
                for h in range(H):
                    last = ET.matmul(PBK[3 + h // 4][:, (h % 4) * 128:(h % 4) * 128 + 128],
                                     kf[h], kf[h], start=True, stop=True)
                tc = inc(last, "sT")
                EV.wait_ge(sems["sT"], tc)
                vc = 0
                for grp in (0, 1):
                    vc = inc(EV.tensor_tensor(
                        AF[:, mATT + grp * 512:mATT + (grp + 1) * 512],
                        PBK[3 + grp][:, 0:512],
                        Lm[:, grp * 512:(grp + 1) * 512], op=Alu.mult), "sV")
                bank_guard[3] = ("sV", vc)
                bank_guard[4] = ("sV", vc)

            w(ET, "sV")
            for (j0, nb) in ((0, 4), (4, 4)):
                use_bank(0)
                last = None
                for j in range(nb):
                    h = j0 + j
                    last = ET.transpose(PBK[0][:, j * 128:(j + 1) * 128],
                                        AF[:, mPP + h * 128:mPP + (h + 1) * 128], idn[:])
                tc = inc(last, "sT")
                ES.wait_ge(sems["sT"], tc)
                ii = ES.activation(AF[:, mPPT + j0 * 128:mPPT + (j0 + nb) * 128],
                                   PBK[0][:, 0:nb * 128], Act.Copy)
                sc = inc(ii, "sS")
                bank_guard[0] = ("sS", sc)
            pp_ok = sc
            rr_mark = 0
            ppt = [af(mPPT)[:, h * 128:(h + 1) * 128] for h in range(H)]
            ppn = [af(mPP)[:, h * 128:(h + 1) * 128] for h in range(H)]
            for lev in range(7):
                if fl_next is not None:
                    next(fl_next, None)
                ET.wait_ge(sems["sS"], pp_ok)
                for b in (1, 2, 5, 6):
                    use_bank(b)
                last = None
                for h in range(H):
                    bnk = [1, 2, 5, 6][h // 2]
                    col = (h % 2) * 256
                    last = ET.matmul(PBK[bnk][:, col:col + 256], ppt[h],
                                     AF[:, mRR + h * 256:mRR + (h + 1) * 256],
                                     start=True, stop=True)
                tc_ap = inc(last, "sT")
                if lev < 6:
                    for b in (3, 4, 0, 7):
                        use_bank(b)
                    last = None
                    for h in range(H):
                        last = ET.matmul(PBK[3 + h // 4][:, (h % 4) * 128:(h % 4) * 128 + 128],
                                         ppt[h], ppn[h], start=True, stop=True)
                    for h in range(H):
                        last = ET.matmul(PBK[(0, 7)[h // 4]][:, (h % 4) * 128:(h % 4) * 128 + 128],
                                         ppn[h], ppt[h], start=True, stop=True)
                    tc_sq = inc(last, "sT")
                    ES.wait_ge(sems["sT"], tc_sq)
                    ES.activation(AF[:, mPP:mPP + 512], PBK[3][:, 0:512], Act.Copy)
                    ES.activation(AF[:, mPP + 512:mPP + 1024], PBK[4][:, 0:512], Act.Copy)
                    ES.activation(AF[:, mPPT:mPPT + 512], PBK[0][:, 0:512], Act.Copy)
                    ii = ES.activation(AF[:, mPPT + 512:mPPT + 1024], PBK[7][:, 0:512],
                                       Act.Copy)
                    pp_ok = inc(ii, "sS")
                    for b in (3, 4, 0, 7):
                        bank_guard[b] = ("sS", pp_ok)
                EV.wait_ge(sems["sT"], tc_ap)
                if rr_mark:
                    EV.wait_ge(sems["sV"], rr_mark)
                last = None
                for bq in range(4):
                    bnk = [1, 2, 5, 6][bq]
                    last = EV.tensor_tensor(AF[:, mRR + bq * 512:mRR + (bq + 1) * 512],
                                            AF[:, mRR + bq * 512:mRR + (bq + 1) * 512],
                                            PBK[bnk][:, 0:512], op=Alu.add)
                vc = inc(last, "sV")
                rr_mark = vc
                for bq in range(4):
                    bank_guard[[1, 2, 5, 6][bq]] = ("sV", vc)
            w(EV, "sV", rr_mark)

            w(ET, "sV")
            for grp in (0, 1):
                use_bank(3 + grp)
            last = None
            for h in range(H):
                last = ET.transpose(PBK[3 + h // 4][:, (h % 4) * 128:(h % 4) * 128 + 128],
                                    AF[:, mRR + h * 256 + 128:mRR + h * 256 + 256], idn[:])
            tc = inc(last, "sT")
            EV.wait_ge(sems["sT"], tc)
            vc = 0
            for grp in (0, 1):
                vc = inc(EV.tensor_copy(AF[:, mKCDT + grp * 512:mKCDT + (grp + 1) * 512],
                                        PBK[3 + grp][:, 0:512]), "sV")
            bank_guard[3] = ("sV", vc)
            bank_guard[4] = ("sV", vc)
            w(ET, "sV", vc)
            for grp in (0, 1):
                use_bank(3 + grp)
            last = None
            for h in range(H):
                last = ET.matmul(PBK[3 + h // 4][:, (h % 4) * 128:(h % 4) * 128 + 128],
                                 AF[:, mKCDT + h * 128:mKCDT + (h + 1) * 128],
                                 Smat[:, h * 128:(h + 1) * 128], start=True, stop=True)
            tc = inc(last, "sT")
            EV.wait_ge(sems["sT"], tc)
            rrv = AF[:, mRR:mRR + 2048].rearrange("p (h t) -> p h t", h=H)
            for grp in (0, 1):
                vnew_ok = inc(EV.scalar_tensor_tensor(
                    af(mVNEW)[:, grp * 512:(grp + 1) * 512].rearrange("p (h d) -> p h d", h=4),
                    PBK[3 + grp][:, 0:512].rearrange("p (h d) -> p h d", h=4), -1.0,
                    rrv[:, grp * 4:(grp + 1) * 4, 0:128],
                    op0=Alu.mult, op1=Alu.add), "sV")
            bank_guard[3] = ("sV", vnew_ok)
            bank_guard[4] = ("sV", vnew_ok)

            if dmp:
                dump("vnew", af(mVNEW))
                if with_o:
                    dump("att", AF[:, mATT:mATT + 1024])
            w(ET, "sV", vnew_ok)
            for grp in (0, 1):
                use_bank(5 + grp)
            last = None
            for h in range(H):
                last = ET.transpose(PBK[5 + h // 4][:, (h % 4) * 128:(h % 4) * 128 + 128],
                                    AF[:, mATT + h * 128:mATT + (h + 1) * 128], idn[:])
            tc = inc(last, "sT")
            EV.wait_ge(sems["sT"], tc)
            attT_ok = 0
            for grp in (0, 1):
                attT_ok = inc(EV.tensor_copy(
                    AF[:, mATTT + grp * 512:mATTT + (grp + 1) * 512],
                    PBK[5 + grp][:, 0:512]), "sV")
            bank_guard[5] = ("sV", attT_ok)
            bank_guard[6] = ("sV", attT_ok)
            if with_o:
                w(ET, "sV", attT_ok)
                for grp in (0, 1):
                    use_bank(5 + grp)
                last = None
                for h in range(H):
                    bnk = 5 + h // 4
                    col = (h % 4) * 128
                    ET.matmul(PBK[bnk][:, col:col + 128],
                              af(mKDEF)[:, h * 128:(h + 1) * 128],
                              Smat[:, h * 128:(h + 1) * 128], start=True, stop=False)
                    last = ET.matmul(PBK[bnk][:, col:col + 128],
                                     AF[:, mATTT + h * 128:mATTT + (h + 1) * 128],
                                     af(mVNEW)[:, h * 128:(h + 1) * 128],
                                     start=False, stop=True)
                tc = inc(last, "sT")
                EV.wait_ge(sems["sT"], tc)
                w(EV, "sSP")
                vc = 0
                for grp in (0, 1):
                    vc = inc(EV.tensor_copy(af(mOT)[:, grp * 512:(grp + 1) * 512],
                                            PBK[5 + grp][:, 0:512]), "sV")
                bank_guard[5] = ("sV", vc)
                bank_guard[6] = ("sV", vc)
                w(EG, "sV", vc)
                inc(EG.dma_start(o_dram[row0:row0 + 128, :], af(mOT)), "sSP", 16)

            w(ET, "sV", vnew_ok)
            for grp in (0, 1):
                use_bank(3 + grp)
            last = None
            for h in range(H):
                last = ET.matmul(PBK[3 + h // 4][:, (h % 4) * 128:(h % 4) * 128 + 128],
                                 af(mKDW)[:, h * 128:(h + 1) * 128],
                                 af(mVNEW)[:, h * 128:(h + 1) * 128], start=True, stop=True)
            tc = inc(last, "sT")
            EV.wait_ge(sems["sT"], tc)
            ms = inc(EV.tensor_tensor(rh3(Smat[:]), rh3(Smat[:]),
                                      ctd_c[:, :, None].broadcast_to((128, H, HD)),
                                      op=Alu.mult), "sQ")
            EV.wait_ge(sems["sQ"], ms)
            vc = 0
            for grp in (0, 1):
                vc = inc(EV.tensor_tensor(Smat[:, grp * 512:(grp + 1) * 512],
                                          Smat[:, grp * 512:(grp + 1) * 512],
                                          PBK[3 + grp][:, 0:512], op=Alu.add), "sV")
            bank_guard[3] = ("sV", vc)
            bank_guard[4] = ("sV", vc)
            if spill:
                w(EG, "sV", attT_ok)
                inc(EG.dma_start(sck_d[row0:row0 + 128, :], af(mSPILL, mSPILLW)),
                    "sSP", 16)
                ck_guard[0] = cnt["sSP"]

        fl = front_lite(xo_d, 0)
        for _ in fl:
            pass
        for n in range(NCH):
            pump_wq(4)
            fl_next = front_lite(xo_d, (n + 1) * C) if n + 1 < NCH else None
            mixer_chunk(n * C, False, None, fl_next=fl_next, spill=True,
                        dmp=(DBG and n == 0))
            if fl_next is not None:
                for _ in fl_next:
                    pass

        BK4 = [1, 2, 7, 0]
        spill_guard = {}

        def xqv(gp, piece, kc, tt, n=128):
            base = (8192 if piece == 0 else 12288) + kc * 256 + tt * 64
            return AF[:, base:base + n // 2].bitcast(BF16)

        xq_ok = [0, 0]
        xq_free = [0, 0]

        def prep_group(g4):
            gp = (g4 // 4) % 2
            ntile = min(4, NT - g4)
            for tt in range(ntile):
                tile = g4 + tt
                w(EY, "sV")
                inc(EY.dma_start(af(fXT), xo_d[tile * 128:(tile + 1) * 128, :]), "sLD3", 16)
                w(EV, "sLD3")
                lv = layernorm(fXT, fXN)
                w(ES, "sV")
                w(EG, "sV")
                w(EV, "sV", lv)
                quant(af(fXN), D, split=False)
                w(ET, "sV")
                w(ET, "sG")
                if xq_free[gp]:
                    ET.wait_ge(sems[rs("sT")], xq_free[gp])

                def evx(tt_):
                    def ev(j0, nb, tc):
                        ES.wait_ge(sems[rs("sT")], tc)
                        sc = 0
                        for j in range(nb):
                            kc = j0 + j
                            sc = inc(ES.activation(xqv(gp, 0, kc, tt_),
                                                   PBK[0][:, j * 128:(j + 1) * 128],
                                                   Act.Copy), "sS")
                        EV.wait_ge(sems[rs("sS")], sc)
                        vc = 0
                        for j in range(nb):
                            kc = j0 + j
                            vc = inc(EV.tensor_tensor(xqv(gp, 1, kc, tt_),
                                                      PBK[0][:, j * 128:(j + 1) * 128],
                                                      xqv(gp, 0, kc, tt_),
                                                      op=Alu.subtract), "sV")
                        bank_guard[0] = (rs("sV"), vc)
                    return ev
                transpose_to(af(qXQ), 8, 0, evx(tt))
                yield
            xq_ok[gp] = cnt[rs("sV")]

        def h_chain(g4, tt, pw, po, up_tc_tt):
            tile = g4 + tt
            EV.wait_ge(sems[rs("sT")], up_tc_tt)
            vc = inc(EV.tensor_tensor(AF[:, fHT + tt * 512:fHT + tt * 512 + pw],
                                      AF[:, fHT + tt * 512:fHT + tt * 512 + pw],
                                      PBK[BK4[tt]][:, 0:pw], op=Alu.mult), "sV")
            bank_guard[BK4[tt]] = (rs("sV"), vc)
            w(EV, "sV", vc)
            par = tt % 2
            psem = "sP0" if par == 0 else "sP1"
            qh_o, ql_o = bQH + par * 512, bQL + par * 512
            if par in spill_guard:
                w(ES, psem, spill_guard[par])
                w(EG, psem, spill_guard[par])
            w(ES, "sV", vc)
            w(EG, "sV", vc)
            quant(AF[:, fHT + tt * 512:fHT + tt * 512 + pw], pw, split=True,
                  qh_off=qh_o, ql_off=ql_o)
            w(EG, "sG")
            inc(EG.dma_start(shq[0][tile * 128:(tile + 1) * 128, po:po + pw],
                             ab(qh_o, pw)), psem, 16)
            inc(EG.dma_start(shq[1][tile * 128:(tile + 1) * 128, po:po + pw],
                             ab(ql_o, pw)), psem, 16)
            spill_guard[par] = cnt[psem]

        def p3a_gen():
            yield from prep_group(0)
            wave_guard = [0, 0]
            for g4 in range(0, NT, 4):
                gp = (g4 // 4) % 2
                ntile = min(4, NT - g4)
                for pan in range(6):
                    pw, po = PANW[pan], PANO[pan]
                    for wi, scrw in [(0, swg), (1, swu)]:
                        wv = (pan * 2 + wi) % 2
                        wsem = "sLD0" if wv == 0 else "sLD1"
                        base = bWP[wv]
                        EY.wait_ge(sems[rs("sT")], wave_guard[wv])
                        for kc in range(8):
                            inc(EY.dma_start_transpose(
                                AB[:, base + kc * 512:base + kc * 512 + pw],
                                scrw[0][po:po + pw, kc * 128:(kc + 1) * 128]), wsem, 16)
                            inc(EY.dma_start_transpose(
                                AB[:, base + 4096 + kc * 512:base + 4096 + kc * 512 + pw],
                                scrw[1][po:po + pw, kc * 128:(kc + 1) * 128]), wsem, 16)
                        w(ET, wsem)
                        if wi == 0:
                            w(ET, "sV", xq_ok[gp])
                        rh = [AB[:, base + kc * 512:base + kc * 512 + pw] for kc in range(8)]
                        rl = [AB[:, base + 4096 + kc * 512:base + 4096 + kc * 512 + pw]
                              for kc in range(8)]
                        up_tc = []
                        for tt in range(ntile):
                            lh = [xqv(gp, 0, kc, tt) for kc in range(8)]
                            ll = [xqv(gp, 1, kc, tt) for kc in range(8)]
                            use_bank(BK4[tt])
                            tc = mm3(BK4[tt], pw, lh, ll, rh, rl)
                            if wi == 0:
                                ES.wait_ge(sems[rs("sT")], tc)
                                par = tt % 2
                                if par in spill_guard:
                                    w(ES, "sP0" if par == 0 else "sP1",
                                      spill_guard[par])
                                sc = inc(ES.activation(
                                    AF[:, fHT + tt * 512:fHT + tt * 512 + pw],
                                    PBK[BK4[tt]][:, 0:pw], Act.Silu), "sS")
                                bank_guard[BK4[tt]] = (rs("sS"), sc)
                            else:
                                up_tc.append(tc)
                                if tt >= 2:
                                    h_chain(g4, tt - 2, pw, po, up_tc[tt - 2])
                        wave_guard[wv] = cnt[rs("sT")]
                        if wi == 1:
                            for tt in range(max(0, ntile - 2), ntile):
                                h_chain(g4, tt, pw, po, up_tc[tt])
                        yield
                if pan == 5 and g4 + 4 < NT:
                    xq_free[0] = cnt[rs("sT")]
                    yield from prep_group(g4 + 4)

        def p3a_redirected():
            gen = p3a_gen()
            while True:
                R["on"] = True
                try:
                    next(gen)
                except StopIteration:
                    R["on"] = False
                    return
                R["on"] = False
                yield

        p3a = p3a_redirected()

        pump_wq(len(wq_slices))
        spill_guard[0] = cnt["sP0"]
        spill_guard[1] = cnt["sP1"]
        EY.wait_ge(sems["sP0"], cnt["sP0"])
        EY.wait_ge(sems["sP1"], cnt["sP1"])
        EY.wait_ge(sems["sSP"], cnt["sSP"])
        w(EV, "sIN")
        xc_ok = inc(EV.tensor_tensor(af(mPP, H * 128), Smat[:],
                                     ismb[:].broadcast_to((128, H * 128)),
                                     op=Alu.mult), "sV")
        w(EG, "sV", xc_ok)
        inc(EG.dma_start(ccin_t.ap(), af(mPP, H * 128)), "sCC", 16)
        w(EG, "sCC")
        cc = inc(EG.collective_compute(
            "AllReduce", Alu.add,
            replica_groups=[[i, i + 1] for i in range(0, n_cores, 2)],
            ins=[ccin_t.ap().opt()],
            outs=[ccout_t.ap().opt()]), "sG")
        EG.wait_ge(sems["sG"], cc)
        inc(EG.dma_start(af(mPP, H * 128), ccout_t.ap()), "sCC", 16)
        w(EG, "sCC")
        g_ok = inc(EG.nop(), "sG")
        w(EV, "sG", g_ok)
        inc(EV.tensor_tensor(Smat[:], af(mPP, H * 128),
                             smb[:].broadcast_to((128, H * 128)), op=Alu.mult), "sV")

        for n in range(NCH):
            next(p3a, None)
            next(p3a, None)
            row0 = n * C
            EY.wait_ge(sems["sT"], cnt["sT"])
            EY.wait_ge(sems["sV"], cnt["sV"])
            inc(EY.dma_start(af(mSPILL, 3072), sck_d[row0:row0 + 128, 0:3072]),
                "sLD", 16)
            ld1 = cnt["sLD"]
            inc(EY.dma_start(af(mSPILL + 3072, mSPILLW - 3072),
                             sck_d[row0:row0 + 128, 3072:mSPILLW]), "sIN", 16)
            ET.wait_ge(sems["sLD"], ld1)
            ld_all = cnt["sIN"]
            for grp in (0, 1):
                use_bank(3 + grp)
            last = None
            for h in range(H):
                last = ET.transpose(PBK[3 + h // 4][:, (h % 4) * 128:(h % 4) * 128 + 128],
                                    AF[:, mRR + h * 256 + 128:mRR + h * 256 + 256], idn[:])
            tc = inc(last, "sT")
            EV.wait_ge(sems["sT"], tc)
            vc = 0
            for grp in (0, 1):
                vc = inc(EV.tensor_copy(AF[:, mKCDT + grp * 512:mKCDT + (grp + 1) * 512],
                                        PBK[3 + grp][:, 0:512]), "sV")
            bank_guard[3] = ("sV", vc)
            bank_guard[4] = ("sV", vc)
            w(ET, "sV", vc)
            for grp in (0, 1):
                use_bank(3 + grp)
            last = None
            for h in range(H):
                last = ET.matmul(PBK[3 + h // 4][:, (h % 4) * 128:(h % 4) * 128 + 128],
                                 AF[:, mKCDT + h * 128:mKCDT + (h + 1) * 128],
                                 Smat[:, h * 128:(h + 1) * 128], start=True, stop=True)
            tc = inc(last, "sT")
            EV.wait_ge(sems["sT"], tc)
            rrv = AF[:, mRR:mRR + 2048].rearrange("p (h t) -> p h t", h=H)
            for grp in (0, 1):
                vnew_ok = inc(EV.scalar_tensor_tensor(
                    af(mVNEW)[:, grp * 512:(grp + 1) * 512].rearrange("p (h d) -> p h d", h=4),
                    PBK[3 + grp][:, 0:512].rearrange("p (h d) -> p h d", h=4), -1.0,
                    rrv[:, grp * 4:(grp + 1) * 4, 0:128],
                    op0=Alu.mult, op1=Alu.add), "sV")
            bank_guard[3] = ("sV", vnew_ok)
            bank_guard[4] = ("sV", vnew_ok)
            w(ET, "sV", vnew_ok)
            ET.wait_ge(sems["sIN"], ld_all)
            for grp in (0, 1):
                use_bank(5 + grp)
            last = None
            for h in range(H):
                bnk = 5 + h // 4
                col = (h % 4) * 128
                ET.matmul(PBK[bnk][:, col:col + 128],
                          af(mKDEF)[:, h * 128:(h + 1) * 128],
                          Smat[:, h * 128:(h + 1) * 128], start=True, stop=False)
                last = ET.matmul(PBK[bnk][:, col:col + 128],
                                 AF[:, mATTT + h * 128:mATTT + (h + 1) * 128],
                                 af(mVNEW)[:, h * 128:(h + 1) * 128],
                                 start=False, stop=True)
            tc = inc(last, "sT")
            EV.wait_ge(sems["sT"], tc)
            w(EV, "sSP")
            vc = 0
            for grp in (0, 1):
                vc = inc(EV.tensor_copy(af(mOT)[:, grp * 512:(grp + 1) * 512],
                                        PBK[5 + grp][:, 0:512]), "sV")
            bank_guard[5] = ("sV", vc)
            bank_guard[6] = ("sV", vc)
            w(EG, "sV", vc)
            inc(EG.dma_start(so_d[row0:row0 + 128, :], af(mOT)), "sSP", 16)
            w(ET, "sV", vnew_ok)
            for grp in (0, 1):
                use_bank(3 + grp)
            last = None
            for h in range(H):
                last = ET.matmul(PBK[3 + h // 4][:, (h % 4) * 128:(h % 4) * 128 + 128],
                                 af(mKDW)[:, h * 128:(h + 1) * 128],
                                 af(mVNEW)[:, h * 128:(h + 1) * 128], start=True, stop=True)
            tc = inc(last, "sT")
            EV.wait_ge(sems["sT"], tc)
            ms = inc(EV.tensor_tensor(rh3(Smat[:]), rh3(Smat[:]),
                                      ctd_c[:, :, None].broadcast_to((128, H, HD)),
                                      op=Alu.mult), "sQ")
            EV.wait_ge(sems["sQ"], ms)
            vc = 0
            for grp in (0, 1):
                vc = inc(EV.tensor_tensor(Smat[:, grp * 512:(grp + 1) * 512],
                                          Smat[:, grp * 512:(grp + 1) * 512],
                                          PBK[3 + grp][:, 0:512], op=Alu.add), "sV")
            bank_guard[3] = ("sV", vc)
            bank_guard[4] = ("sV", vc)

        for _ in p3a:
            pass
        def afb(c0, n):
            return AF[:, c0:c0 + n].bitcast(BF16)

        EY.wait_ge(sems["sT"], cnt["sT"])
        EY.wait_ge(sems["sSP"], cnt["sSP"])
        EY.wait_ge(sems["sP0"], cnt["sP0"])
        EY.wait_ge(sems["sP1"], cnt["sP1"])
        inc(EY.dma_start(WT_h[:, :], swt[0][:, :]), "sLD", 16)
        inc(EY.dma_start(WT_l[:, :], swt[1][:, :]), "sLD", 16)
        stage_free = [0, 0]
        fm_ready = [0, 0]
        g_guard = [0, 0]
        ot_free = [0, 0]
        ld3_mark = [0, 0]
        xt_guard = [0]
        for hp in range(2):
            EY.wait_ge(sems["sT"], cnt["sT"])
            EY.wait_ge(sems["sSP"], cnt["sSP"])
            EY.wait_ge(sems["sP0"], cnt["sP0"])
            EY.wait_ge(sems["sP1"], cnt["sP1"])
            g_guard = [0, 0]
            for ic in range(NIC):
                inc(EY.dma_start_transpose(
                    AB[:, bWD_H + ic * 512:bWD_H + ic * 512 + 512],
                    swd[0][hp * 512:(hp + 1) * 512, ic * 128:(ic + 1) * 128]), "sLD", 16)
                inc(EY.dma_start_transpose(
                    AB[:, bWD_L + ic * 512:bWD_L + ic * 512 + 512],
                    swd[1][hp * 512:(hp + 1) * 512, ic * 128:(ic + 1) * 128]), "sLD", 16)
            for tile in range(NT):
                par = tile % 2
                psem2p = "sP0" if par == 0 else "sP1"
                if hp == 1:
                    pGHp = fGH if par == 0 else fGH2
                    pOTp = fOT if par == 0 else fOT2
                    if g_guard[par]:
                        EY.wait_ge(sems[psem2p], g_guard[par])
                    if ot_free[par]:
                        EY.wait_ge(sems["sT"], ot_free[par])
                    if xt_guard[0]:
                        EY.wait_ge(sems["sV"], xt_guard[0])
                    inc(EY.dma_start(AF[:, pGHp + 512:pGHp + 1024],
                                     sg_d[tile * 128:(tile + 1) * 128, :]), "sLD3", 16)
                    inc(EY.dma_start(af(pOTp), so_d[tile * 128:(tile + 1) * 128, :]),
                        "sLD3", 16)
                    inc(EY.dma_start(af(fXT), xo_d[tile * 128:(tile + 1) * 128, :]),
                        "sLD3", 16)
                    ld3_mark[par] = cnt["sLD3"]
                EY.wait_ge(sems["sT"], stage_free[par])
                hsem = "sLD0" if par == 0 else "sLD1"
                inc(EY.dma_start(afb(fSTG[par][0], 1408), shq[0][tile * 128:(tile + 1) * 128, :]),
                    hsem, 16)
                inc(EY.dma_start(afb(fSTG[par][1], 1408), shq[1][tile * 128:(tile + 1) * 128, :]),
                    hsem, 16)
                hq_lc = cnt[hsem]
                ET.wait_ge(sems[hsem], hq_lc)
                src_list = [(fSTG[par][pc], fFM[par][pc]) for pc in range(2)]
                blocks = [(pc, ic) for pc in range(2) for ic in range(NIC)]
                bi = 0
                while bi < len(blocks):
                    nb = min(8, len(blocks) - bi)
                    use_bank(0)
                    last = None
                    for j in range(nb):
                        pc, ic = blocks[bi + j]
                        last = ET.transpose(
                            PBK[0][:, j * 64:j * 64 + 64].bitcast(BF16),
                            afb(fSTG[par][pc], 1408)[:, ic * 128:(ic + 1) * 128],
                            idnb[:])
                    tc = inc(last, "sT")
                    EV.wait_ge(sems["sT"], tc)
                    vc = 0
                    for j in range(nb):
                        pc, ic = blocks[bi + j]
                        vc = inc(EV.tensor_copy(
                            afb(fFM[par][pc], 1408)[:, ic * 128:(ic + 1) * 128],
                            PBK[0][:, j * 64:j * 64 + 64].bitcast(BF16)), "sV")
                    bank_guard[0] = ("sV", vc)
                    bi += nb
                stage_free[par] = cnt["sT"]
                fm_ready[par] = cnt["sV"]
                w(ET, "sLD")
                w(ET, "sV", fm_ready[par])
                psb = 1 + par
                use_bank(psb)
                lh = [afb(fFM[par][0], 1408)[:, ic * 128:(ic + 1) * 128] for ic in range(NIC)]
                ll = [afb(fFM[par][1], 1408)[:, ic * 128:(ic + 1) * 128] for ic in range(NIC)]
                rh = [AB[:, bWD_H + ic * 512:bWD_H + (ic + 1) * 512] for ic in range(NIC)]
                rl = [AB[:, bWD_L + ic * 512:bWD_L + (ic + 1) * 512] for ic in range(NIC)]
                tc = mm3(psb, 512, lh, ll, rh, rl)
                ES.wait_ge(sems["sT"], tc)
                pXN = fXN if par == 0 else fXN2
                pST = fST if par == 0 else fST2
                pGH = fGH if par == 0 else fGH2
                pOT = fOT if par == 0 else fOT2
                pOF = bOFH if par == 0 else bOF2
                psem2 = "sP0" if par == 0 else "sP1"
                if hp == 0:
                    if g_guard[par]:
                        w(ES, psem2, g_guard[par])
                    sc = inc(ES.activation(af(pGH, 512), PBK[psb][:, 0:512],
                                           Act.Silu), "sS")
                    bank_guard[psb] = ("sS", sc)
                    w(EG, "sS", sc)
                    inc(EG.dma_start(sg_d[tile * 128:(tile + 1) * 128, :], af(pGH, 512)),
                        psem2, 16)
                    g_guard[par] = cnt[psem2]
                else:
                    if g_guard[par]:
                        w(ES, psem2, g_guard[par])
                    sc = inc(ES.activation(af(pGH, 512), PBK[psb][:, 0:512],
                                           Act.Silu), "sS")
                    bank_guard[psb] = ("sS", sc)
                    w(EV, "sLD3", ld3_mark[par])
                    ln_ok = layernorm(fXT, pXN, sq_off=fSQ3)
                    ot_free_pending = True
                    xt_guard[0] = cnt["sV"]
                    w(ET, "sV")
                    transpose_to(af(pOT), 8, 0, evac_split(AB[:, pOF:pOF + D],
                                                           AB[:, pOF + 1024:pOF + 1024 + D]
                                                           if False else
                                                           AB[:, pOF + 1024:pOF + 2048], 0))
                    of_ok = cnt["sV"]
                    ot_free[par] = cnt["sT"]
                    lh2 = [AB[:, pOF + kc * 128:pOF + (kc + 1) * 128] for kc in range(8)]
                    ll2 = [AB[:, pOF + 1024 + kc * 128:pOF + 1024 + (kc + 1) * 128]
                           for kc in range(8)]
                    w(ET, "sV", of_ok)
                    w(ET, "sLD")
                    rtc = []
                    for pan2 in (0, 1):
                        use_bank(5 + pan2)
                        rh2 = [WT_h[:, kc * D + pan2 * 512:kc * D + pan2 * 512 + 512]
                               for kc in range(8)]
                        rl2 = [WT_l[:, kc * D + pan2 * 512:kc * D + pan2 * 512 + 512]
                               for kc in range(8)]
                        rtc.append(mm3(5 + pan2, 512, lh2, ll2, rh2, rl2))
                    EV.wait_ge(sems["sT"], rtc[-1])
                    EV.wait_ge(sems["sV"], ln_ok)
                    vc = 0
                    for pan2 in (0, 1):
                        vc = inc(EV.tensor_tensor(
                            AF[:, pST + pan2 * 512:pST + (pan2 + 1) * 512],
                            af(pXN)[:, pan2 * 512:(pan2 + 1) * 512],
                            PBK[5 + pan2][:, 0:512], op=Alu.add), "sV")
                    bank_guard[5] = ("sV", vc)
                    bank_guard[6] = ("sV", vc)
                    w(EV, "sS", sc)
                    EV.wait_ge(sems["sV"], vc)
                    mg = 0
                    mg = inc(EV.tensor_tensor(AF[:, pST:pST + 512], AF[:, pST:pST + 512],
                                              AF[:, pGH + 512:pGH + 1024],
                                              op=Alu.mult), "sQ")
                    mg = inc(EV.tensor_tensor(AF[:, pST + 512:pST + 1024],
                                              AF[:, pST + 512:pST + 1024],
                                              AF[:, pGH:pGH + 512], op=Alu.mult), "sQ")
                    EV.wait_ge(sems["sQ"], mg)
                    last = EV.tensor_tensor(AF[:, pST:pST + 1024], AF[:, pST:pST + 1024],
                                            af(fXT), op=Alu.add)
                    vc = inc(last, "sV")
                    xt_guard[0] = vc
                    w(EG, "sV", vc)
                    inc(EG.dma_start(y_d[tile * 128:(tile + 1) * 128, :],
                                     AF[:, pST:pST + 1024]), psem2, 16)
                    g_guard[par] = cnt[psem2]

        EG.wait_ge(sems["sSP"], cnt["sSP"])
        EG.wait_ge(sems["sP0"], cnt["sP0"])
        EG.wait_ge(sems["sP1"], cnt["sP1"])
    return nc


def kernel(**inputs):
    xs = np.ascontiguousarray(inputs["L0_attn_out"], dtype=np.float32)
    B, T, _ = xs.shape
    t_own = T // 2
    in_maps = []
    for core in range(8):
        b, half = core // 2, core % 2
        own = xs[b, half * t_own:(half + 1) * t_own]
        in_maps.append({
            "xo": np.ascontiguousarray(own),
            "smask": np.full((1, 1), 1.0 if half == 1 else 0.0, np.float32),
            "Ww": np.ascontiguousarray(inputs["L0_attn_W_write"], dtype=np.float32),
            "Wr": np.ascontiguousarray(inputs["L0_attn_W_read"], dtype=np.float32),
            "Wb": np.ascontiguousarray(inputs["L0_attn_W_beta"], dtype=np.float32),
            "dec": np.ascontiguousarray(inputs["L0_attn_decay"], dtype=np.float32).reshape(1, H),
            "wg": np.ascontiguousarray(inputs["L0_ffn_w_gate"], dtype=np.float32),
            "wu": np.ascontiguousarray(inputs["L0_ffn_w_up"], dtype=np.float32),
            "wd": np.ascontiguousarray(inputs["L0_ffn_w_down"], dtype=np.float32),
        })
    nc = build_nc(t_own=t_own, n_cores=8)
    res = run_bass_kernel_spmd(nc, in_maps, core_ids=list(range(8)))
    out = np.zeros((B, T, D), np.float32)
    for core in range(8):
        b, half = core // 2, core % 2
        out[b, half * t_own:(half + 1) * t_own] = res.results[core]["y"]
    return out


# version 25
# speedup vs baseline: 1.0135x; 1.0105x over previous
import numpy as np
import ml_dtypes
import concourse.bass as bass
import concourse.mybir as mybir
from concourse.bass_utils import run_bass_kernel_spmd
from contextlib import ExitStack

F32 = mybir.dt.float32
BF16 = mybir.dt.bfloat16
I32 = mybir.dt.int32
Act = mybir.ActivationFunctionType
Alu = mybir.AluOpType
AX = mybir.AxisListType

D, H, HD, C, IF = 1024, 8, 128, 128, 2816
NKC = D // 128
NIC = IF // 128
MAGIC = 12582912.0
VELT = float(2**22 + 1)
DBG = False
PANW = [512, 512, 512, 512, 512, 256]
PANO = [0, 512, 1024, 1536, 2048, 2560]

qU, qC1, qT1, qXQ = 0, 1024, 2048, 3072
wLD = [4096, 5120]
qM, qP, qR = 6144, 6400, 6656
mXT, mXN, mKT, mKBT, mOT = 7168, 8192, 9216, 10240, 11264
mKFM, mKBFM, mKDEF = 12288, 13312, 14336
mRR = 15360
mATTT, mKDW, mATT = 17408, 18432, 19456
mPP, mPPT = 20480, 21504
mKCDT, mVNEW = mPPT, mPP
mSPILL, mSPILLW = 14336, 5120
fHT = 4096
fXT, fXN = 6144, 7168
fST, fGH, fOT = 8192, 9216, 10240
fSQ3 = 16896
fXN2, fST2, fGH2, fOT2 = 17920, 18944, 19968, 20992
bOF2 = 22528

bTH, bLO, bM, bSEL, bQH, bQL = 0, 1024, 2048, 3072, 4096, 5120
bXQH, bXQL = 6144, 10240
bWP = [14336, 22528]
bXFH, bXFL = 6144, 7168
bWD_H, bWD_L = 0, 11264
bOFH, bOFL = 28160, 29184
fSTG = [[0, 1408], [2816, 4224]]
fFM = [[11264, 12672], [14080, 15488]]


def build_nc(t_own=2048, t_pre=2048, n_cores=8):
    nc = bass.Bass("TRN2", target_bir_lowering=False)
    NT = t_own // 128
    NCH = t_own // C
    NCHP = t_pre // C

    xo_d = nc.dram_tensor("xo", [t_own, D], F32, kind="ExternalInput").ap()
    sm_d = nc.dram_tensor("smask", [1, 1], F32, kind="ExternalInput").ap()
    Ww_d = nc.dram_tensor("Ww", [D, D], F32, kind="ExternalInput").ap()
    Wr_d = nc.dram_tensor("Wr", [D, D], F32, kind="ExternalInput").ap()
    Wb_d = nc.dram_tensor("Wb", [H, D], F32, kind="ExternalInput").ap()
    dec_d = nc.dram_tensor("dec", [1, H], F32, kind="ExternalInput").ap()
    wg_d = nc.dram_tensor("wg", [IF, D], F32, kind="ExternalInput").ap()
    wu_d = nc.dram_tensor("wu", [IF, D], F32, kind="ExternalInput").ap()
    wd_d = nc.dram_tensor("wd", [D, IF], F32, kind="ExternalInput").ap()
    y_d = nc.dram_tensor("y", [t_own, D], F32, kind="ExternalOutput").ap()
    swg = [nc.dram_tensor(f"swg{p}", [IF, D], BF16, kind="Internal").ap() for p in range(2)]
    swu = [nc.dram_tensor(f"swu{p}", [IF, D], BF16, kind="Internal").ap() for p in range(2)]
    swd = [nc.dram_tensor(f"swd{p}", [D, IF], BF16, kind="Internal").ap() for p in range(2)]
    swt = [nc.dram_tensor(f"swt{p}", [128, NKC * D], BF16, kind="Internal").ap() for p in range(2)]
    shq = [nc.dram_tensor(f"shq{p}", [t_own, IF], BF16, kind="Internal").ap() for p in range(2)]
    so_d = nc.dram_tensor("so", [t_own, D], F32, kind="Internal").ap()
    sck_d = nc.dram_tensor("sck", [t_own, 5120], F32, kind="Internal").ap()
    ccin_t = nc.dram_tensor("ccin", [128, H * 128], F32)
    ccout_t = nc.dram_tensor("ccout", [128, H * 128], F32)
    dbg = {}
    if DBG:
        for nm, wdt in [("xn", D), ("kt", D), ("R0", 2048), ("A0", 1024),
                        ("RN", 2048), ("att", 1024), ("vnew", 1024), ("S1", 1024),
                        ("S2", 1024), ("bet", H), ("kdw", D), ("kdef", D),
                        ("dwc", H), ("dec2", H), ("ctdc", H), ("Lmt", 1024),
                        ("nLst", 1024), ("mtit", 128)]:
            dbg[nm] = nc.dram_tensor("dbg_" + nm, [128, wdt], F32, kind="Internal").ap()
    sg_d = nc.dram_tensor("sgh", [t_own, 512], F32, kind="Internal").ap()

    with ExitStack() as S_:
        sb = lambda n, s, d: S_.enter_context(nc.sbuf_tensor(n, s, d))
        idn = sb("idn", [128, 128], F32)
        idnb = sb("idnb", [128, 128], BF16)
        iot = sb("iot", [128, 128], I32)
        irow = sb("irow", [128, 128], I32)
        idf = sb("idf", [128, 128], F32)
        irf = sb("irf", [128, 128], F32)
        mti = sb("mti", [128, 128], F32)
        mts = sb("mts", [128, 128], F32)
        lgb = sb("lgb", [128, H], F32)
        smb = sb("smb", [128, 1], F32)
        ismb = sb("ismb", [128, 1], F32)
        de_c = sb("de_c", [128, H], F32)
        dw_c = sb("dw_c", [128, H], F32)
        ctd_c = sb("ctd_c", [128, H], F32)
        de_f = sb("de_f", [128, H * 128], F32)
        Lm = sb("Lm", [128, H * 128], F32)
        nLs = sb("nLs", [128, H * 128], F32)
        Smat = sb("Smat", [128, H * 128], F32)
        colv = sb("colv", [128, 1], F32)
        rn = sb("rn", [128, H], F32)
        bet = sb("bet", [128, H], F32)
        lnm = sb("lnm", [128, 1], F32)
        lnv = sb("lnv", [128, 1], F32)
        WT_h = sb("WT_h", [128, NKC * D], BF16)
        WT_l = sb("WT_l", [128, NKC * D], BF16)
        WbT_h = sb("WbT_h", [128, NKC * H], BF16)
        WbT_l = sb("WbT_l", [128, NKC * H], BF16)
        AF = sb("AF", [128, 22528], F32)
        AB = sb("AB", [128, 30720], BF16)
        PBK = [S_.enter_context(nc.psum_tensor(f"PB{i}", [128, 512], F32)) for i in range(8)]

        sems, cnt = {}, {}
        for s in ["sLD", "sLD0", "sLD1", "sLD3", "sIN", "sSP", "sP0", "sP1", "sCC", "sV", "sS", "sT", "sG",
                  "sV2", "sS2", "sT2", "sG2", "sQ", "sQS"]:
            sems[s] = S_.enter_context(nc.semaphore(s))
            cnt[s] = 0

        blk = S_.enter_context(nc.Block())
        EV, ES, ET, EG, EY = nc.vector, nc.scalar, nc.tensor, nc.gpsimd, nc.sync

        R = {"on": False}

        def rs(s):
            return s + "2" if (R["on"] and s in ("sT", "sV", "sS", "sG")) else s

        def inc(inst, s, k=1):
            s = rs(s)
            inst.then_inc(sems[s], k)
            cnt[s] += k
            return cnt[s]

        def w(eng, s, val=None):
            s = rs(s)
            eng.wait_ge(sems[s], cnt[s] if val is None else val)

        def vmark():
            EV.drain()
            return inc(EV.nop(), "sV")

        def smark():
            ES.drain()
            return inc(ES.nop(), "sS")

        bank_guard = {}

        def use_bank(i):
            g = bank_guard.pop(i, None)
            if g is not None:
                ET.wait_ge(sems[g[0]], g[1])

        def af(off, n=1024):
            return AF[:, off:off + n]

        def ab(off, n=1024):
            return AB[:, off:off + n]

        def r3(apx, k=16):
            return apx.rearrange("p (b k) -> p b k", k=k)

        def rh3(apx):
            return apx.rearrange("p (h d) -> p h d", h=H)

        def dump(nm, src_ap):
            if not DBG:
                return
            w(EG, "sV")
            w(EG, "sS")
            inc(EG.dma_start(dbg[nm][:, :], src_ap), "sSP", 16)
            w(EG, "sSP")
            w(EV, "sSP")
            w(ES, "sSP")

        qlast = [0]

        def quant(src, F, split=True, qh_off=None, ql_off=None):
            NB = F // 16
            qh_off = bQH if qh_off is None else qh_off
            ql_off = bQL if ql_off is None else ql_off
            pm, pp, pr = AF[:, qM:qM + NB], AF[:, qP:qP + NB], AF[:, qR:qR + NB]
            if qlast[0]:
                EV.wait_ge(sems[qlast[0][0]], qlast[0][1])
            F2, NB2 = F // 2, NB // 2
            sh = [src[:, 0:F2], src[:, F2:F]]

            def hv(off, h):
                return AF[:, off + h * F2:off + (h + 1) * F2]

            def hb(off, h):
                return AB[:, off + h * F2:off + (h + 1) * F2]

            pmh = [pm[:, 0:NB2], pm[:, NB2:NB]]
            pph = [pp[:, 0:NB2], pp[:, NB2:NB]]
            prh = [pr[:, 0:NB2], pr[:, NB2:NB]]
            mk = {}
            for h in (0, 1):
                mk["r", h] = inc(EV.tensor_reduce(
                    pmh[h], r3(sh[h]), axis=AX.X, op=Alu.max,
                    apply_absolute_value=True), "sQ")
            for h in (0, 1):
                EV.wait_ge(sems["sQ"], mk["r", h])
                mk["p", h] = inc(EV.tensor_scalar(
                    pph[h], pmh[h], 1.0 / 12.0, 5e-13,
                    op0=Alu.mult, op1=Alu.max), "sQ")
            for h in (0, 1):
                EV.wait_ge(sems["sQ"], mk["p", h])
                mk["i", h] = inc(EV.reciprocal(prh[h], pph[h]), "sQ")
            u_ok = 0
            for h in (0, 1):
                EV.wait_ge(sems["sQ"], mk["i", h])
                u_ok = inc(EV.tensor_tensor(
                    r3(hv(qU, h)), r3(sh[h]),
                    prh[h][:, :, None].broadcast_to((128, NB2, 16)),
                    op=Alu.mult), "sV")
            w(ES, "sV", u_ok)
            xq_f = inc(ES.activation(af(qXQ, F), af(qU, F), Act.Copy, bias=MAGIC), "sQS")
            ES.activation(af(qC1, F), af(qU, F), Act.Copy, scale=VELT)
            ES.wait_ge(sems["sQS"], xq_f)
            lo_ok = inc(ES.activation(ab(bLO, F), af(qXQ, F), Act.Copy, bias=-MAGIC), "sS")
            w(EV, "sS", lo_ok)
            t1_f = inc(EV.tensor_tensor(af(qT1, F), af(qC1, F), af(qU, F),
                                        op=Alu.subtract), "sQ")
            ab_f = inc(EV.tensor_scalar(af(qXQ, F).bitcast(I32), af(qU, F).bitcast(I32),
                                        0x7FFFFFFF, None, op0=Alu.bitwise_and), "sQ")
            EV.wait_ge(sems["sQ"], t1_f)
            th_f = inc(EV.tensor_tensor(ab(bTH, F), af(qC1, F), af(qT1, F),
                                        op=Alu.subtract), "sQ")
            EV.wait_ge(sems["sQ"], ab_f)
            m_f = inc(EV.tensor_scalar(ab(bM, F), af(qXQ, F), 1.75, None,
                                       op0=Alu.is_gt), "sQ")
            EV.wait_ge(sems["sQ"], th_f)
            s1 = {}
            for h in (0, 1):
                s1[h] = inc(EV.tensor_tensor(hb(bSEL, h), hb(bTH, h), hb(bLO, h),
                                             op=Alu.subtract), "sQ")
            EV.wait_ge(sems["sQ"], m_f)
            s2 = {}
            for h in (0, 1):
                EV.wait_ge(sems["sQ"], s1[h])
                s2[h] = inc(EV.tensor_tensor(hb(bSEL, h), hb(bM, h), hb(bSEL, h),
                                             op=Alu.mult), "sQ")
            sel_ok = 0
            for h in (0, 1):
                EV.wait_ge(sems["sQ"], s2[h])
                sel_ok = inc(EV.tensor_tensor(hb(bSEL, h), hb(bLO, h), hb(bSEL, h),
                                              op=Alu.add), "sV")
            w(EG, "sV", sel_ok)
            xq_ok = inc(EG.tensor_tensor(r3(af(qXQ, F)), r3(ab(bSEL, F)),
                        pp[:, :, None].broadcast_to((128, NB, 16)), op=Alu.mult), "sG")
            if split:
                w(ES, "sG", xq_ok)
                qh_ok = inc(ES.activation(ab(qh_off, F), af(qXQ, F), Act.Copy), "sS")
                w(EG, "sS", qh_ok)
                xq_ok = inc(EG.tensor_tensor(ab(ql_off, F), af(qXQ, F), ab(qh_off, F),
                                             op=Alu.subtract), "sG")
            qlast[0] = (rs("sG"), xq_ok)
            return xq_ok

        def layernorm(src_off, dst_off, sq_off=None):
            sq_off = src_off if sq_off is None else sq_off
            q1 = inc(EV.tensor_reduce(lnm[:], af(src_off), axis=AX.X, op=Alu.add), "sQ")
            EV.wait_ge(sems["sQ"], q1)
            q2 = inc(EV.tensor_scalar(lnm[:], lnm[:], 1.0 / D, None, op0=Alu.mult), "sQ")
            EV.wait_ge(sems["sQ"], q2)
            v1 = inc(EV.tensor_scalar(af(dst_off), af(src_off), lnm[:], None,
                                      op0=Alu.subtract), "sV")
            w(ES, "sV", v1)
            s1 = inc(ES.activation(af(sq_off), af(dst_off), Act.Square), "sS")
            w(EV, "sS", s1)
            q3 = inc(EV.tensor_reduce(lnv[:], af(sq_off), axis=AX.X, op=Alu.add), "sQ")
            EV.wait_ge(sems["sQ"], q3)
            v2 = inc(EV.tensor_scalar(lnv[:], lnv[:], 1.0 / D, 1e-5,
                                      op0=Alu.mult, op1=Alu.add), "sV")
            w(ES, "sV", v2)
            s2 = inc(ES.activation(lnv[:], lnv[:], Act.Sqrt), "sS")
            w(EV, "sS", s2)
            mr = inc(EV.reciprocal(lnv[:], lnv[:]), "sQ")
            EV.wait_ge(sems["sQ"], mr)
            return inc(EV.tensor_scalar(af(dst_off), af(dst_off), lnv[:], None,
                                        op0=Alu.mult), "sV")

        def tr_blocks(src, nblk, bank):
            batches = []
            done = 0
            while done < nblk:
                nb = min(4, nblk - done)
                batches.append((done, nb))
                done += nb
            return batches

        def transpose_to(src, nblk, bank, evac):
            for (j0, nb) in tr_blocks(src, nblk, bank):
                use_bank(bank)
                last = None
                for j in range(nb):
                    last = ET.transpose(PBK[bank][:, j * 128:(j + 1) * 128],
                                        src[:, (j0 + j) * 128:(j0 + j + 1) * 128], idn[:])
                tc = inc(last, "sT")
                evac(j0, nb, tc)

        def evac_split(dsth, dstl, bank):
            def ev(j0, nb, tc):
                ES.wait_ge(sems["sT"], tc)
                sc = inc(ES.activation(dsth[:, j0 * 128:(j0 + nb) * 128],
                                       PBK[bank][:, 0:nb * 128], Act.Copy), "sS")
                EV.wait_ge(sems["sS"], sc)
                vc = inc(EV.tensor_tensor(dstl[:, j0 * 128:(j0 + nb) * 128],
                                          PBK[bank][:, 0:nb * 128],
                                          dsth[:, j0 * 128:(j0 + nb) * 128],
                                          op=Alu.subtract), "sV")
                bank_guard[bank] = (rs("sV"), vc)
            return ev

        def evac_copy(dst, bank):
            def ev(j0, nb, tc):
                EV.wait_ge(sems["sT"], tc)
                vc = inc(EV.tensor_copy(dst[:, j0 * 128:(j0 + nb) * 128],
                                        PBK[bank][:, 0:nb * 128]), "sV")
                bank_guard[bank] = (rs("sV"), vc)
            return ev

        def mm3(bank, ncol, lh, ll, rh, rl):
            last = None
            n = len(lh)
            for i in range(n):
                last = ET.matmul(PBK[bank][:, 0:ncol], lh[i], rh[i],
                                 start=(i == 0), stop=False)
                last = ET.matmul(PBK[bank][:, 0:ncol], lh[i], rl[i],
                                 start=False, stop=False)
                last = ET.matmul(PBK[bank][:, 0:ncol], ll[i], rh[i],
                                 start=False, stop=(i == n - 1))
            return inc(last, "sT")

        inc(EY.dma_start(lgb[:], dec_d[0:1, :].partition_broadcast(128)), "sIN", 16)
        inc(EY.dma_start(smb[:], sm_d[0:1, :].partition_broadcast(128)), "sIN", 16)
        EG.iota(iot[:], [[-1, 128]], base=0, channel_multiplier=1)
        EG.iota(irow[:], [[1, 128]], base=1, channel_multiplier=0)
        EG.drain()
        EG.tensor_scalar(idn[:], iot[:], 0.0, None, op0=Alu.is_equal)
        EG.drain()
        EG.tensor_copy(idnb[:], idn[:])
        EG.tensor_copy(idf[:], iot[:])
        EG.tensor_copy(irf[:], irow[:])
        EG.drain()
        g0 = inc(EG.nop(), "sV")
        w(EV, "sIN")
        EV.tensor_scalar(ismb[:], smb[:], -1.0, 1.0, op0=Alu.mult, op1=Alu.add)
        EV.drain()
        w(ES, "sIN")
        ES.activation(lgb[:], lgb[:], Act.Sigmoid)
        ES.drain()
        ES.activation(lgb[:], lgb[:], Act.Ln)
        lg_ok = smark()
        w(EV, "sV", g0)
        EV.tensor_scalar(mti[:], idf[:], -0.5, None, op0=Alu.is_gt)
        EV.tensor_scalar(mts[:], idf[:], 0.5, None, op0=Alu.is_gt)
        w(EV, "sS", lg_ok)
        EV.drain()
        EV.tensor_tensor(rh3(Lm[:]), idf[:, None, :].broadcast_to((128, H, 128)),
                         lgb[:, :, None].broadcast_to((128, H, 128)), op=Alu.mult)
        EV.drain()
        EV.tensor_tensor(rh3(de_f[:]), irf[:, None, :].broadcast_to((128, H, 128)),
                         lgb[:, :, None].broadcast_to((128, H, 128)), op=Alu.mult)
        v1 = vmark()
        w(ES, "sV", v1)
        ES.activation(Lm[:], Lm[:], Act.Exp)
        ES.activation(de_f[:], de_f[:], Act.Exp)
        s1 = smark()
        w(EV, "sS", s1)
        EV.tensor_tensor(rh3(Lm[:]), rh3(Lm[:]),
                         mti[:, None, :].broadcast_to((128, H, 128)), op=Alu.mult)
        EV.drain()
        EV.tensor_tensor(rh3(nLs[:]), rh3(Lm[:]),
                         mts[:, None, :].broadcast_to((128, H, 128)), op=Alu.mult)
        EV.drain()
        EV.tensor_scalar(nLs[:], nLs[:], -1.0, None, op0=Alu.mult)
        EV.tensor_scalar(colv[:], idf[:, 0:1], 1.0, None, op0=Alu.add)
        EV.drain()
        EV.tensor_tensor(de_c[:], colv[:].broadcast_to((128, H)), lgb[:, 0:H], op=Alu.mult)
        EV.drain()
        EV.tensor_scalar(colv[:], idf[:, 0:1], -1.0, float(C - 1), op0=Alu.mult, op1=Alu.add)
        EV.drain()
        EV.tensor_tensor(dw_c[:], colv[:].broadcast_to((128, H)), lgb[:, 0:H], op=Alu.mult)
        EV.drain()
        EV.memset(colv[:], float(C))
        EV.drain()
        EV.tensor_tensor(ctd_c[:], colv[:].broadcast_to((128, H)), lgb[:, 0:H], op=Alu.mult)
        v2 = vmark()
        w(ES, "sV", v2)
        ES.activation(de_c[:], de_c[:], Act.Exp)
        ES.activation(dw_c[:], dw_c[:], Act.Exp)
        ES.activation(ctd_c[:], ctd_c[:], Act.Exp)
        tables_ok_s = smark()
        EV.memset(Smat[:], 0.0)
        smat0 = vmark()
        if DBG:
            dump("dwc", dw_c[:])
            dump("dec2", de_c[:])
            dump("ctdc", ctd_c[:])
            dump("Lmt", Lm[:])
            dump("nLst", nLs[:])
            dump("mtit", mti[:])

        wq_slices = []
        for (src, dh, dl, rows, colsF) in [
            (wg_d, swg[0], swg[1], IF, D),
            (wu_d, swu[0], swu[1], IF, D),
            (wd_d, swd[0], swd[1], D, IF),
        ]:
            for r0 in range(0, rows, 128):
                for f0 in range(0, colsF, 1024):
                    wq_slices.append((src, dh, dl, r0, f0, min(1024, colsF - f0)))
        wq_state = {"j": 0, "buf_free": [0, 0], "spill": [0, 0], "t0": None}

        def pump_wq(nj=1):
            for _ in range(nj):
                j = wq_state["j"]
                if j >= len(wq_slices):
                    return
                if wq_state["t0"] is None:
                    wq_state["t0"] = cnt["sT"]
                wq_state["j"] = j + 1
                (src, dh, dl, r0, f0, F) = wq_slices[j]
                par = j % 2
                lsem = "sLD0" if par == 0 else "sLD1"
                psem = "sP0" if par == 0 else "sP1"
                qh_o = 4096 if par == 0 else 8192
                ql_o = 5120 if par == 0 else 9216
                if wq_state["buf_free"][par]:
                    EY.wait_ge(sems["sV"], wq_state["buf_free"][par])
                else:
                    EY.wait_ge(sems["sT"], wq_state["t0"])
                inc(EY.dma_start(af(wLD[par], F), src[r0:r0 + 128, f0:f0 + F]), lsem, 16)
                w(EV, lsem)
                if wq_state["spill"][par]:
                    w(ES, psem, wq_state["spill"][par])
                    w(EG, psem, wq_state["spill"][par])
                quant(af(wLD[par], F), F, split=True, qh_off=qh_o, ql_off=ql_o)
                w(EG, "sG")
                inc(EG.dma_start(dh[r0:r0 + 128, f0:f0 + F], ab(qh_o, F)), psem, 16)
                inc(EG.dma_start(dl[r0:r0 + 128, f0:f0 + F], ab(ql_o, F)), psem, 16)
                wq_state["spill"][par] = cnt[psem]
                wq_state["buf_free"][par] = cnt["sV"]

        for (Wsrc, spill) in [(Wr_d, True), (Ww_d, False)]:
            for r0 in range(0, D, 128):
                EY.wait_ge(sems["sT"], cnt["sT"])
                inc(EY.dma_start(af(wLD[0], D), Wsrc[r0:r0 + 128, :]), "sLD", 16)
                w(ET, "sLD")

                def mk(dh_, dl_, r0_):
                    def ev(j0, nb, tc):
                        ES.wait_ge(sems["sT"], tc)
                        sc = 0
                        for jj in range(nb):
                            kc = j0 + jj
                            sc = inc(ES.activation(dh_[:, kc * D + r0_:kc * D + r0_ + 128],
                                                   PBK[3][:, jj * 128:(jj + 1) * 128],
                                                   Act.Copy), "sS")
                        EV.wait_ge(sems["sS"], sc)
                        vc = 0
                        for jj in range(nb):
                            kc = j0 + jj
                            vc = inc(EV.tensor_tensor(
                                dl_[:, kc * D + r0_:kc * D + r0_ + 128],
                                PBK[3][:, jj * 128:(jj + 1) * 128],
                                dh_[:, kc * D + r0_:kc * D + r0_ + 128],
                                op=Alu.subtract), "sV")
                        bank_guard[3] = ("sV", vc)
                    return ev
                transpose_to(af(wLD[0], D), 8, 3, mk(WT_h, WT_l, r0))
            if spill:
                w(EG, "sV")
                inc(EG.dma_start(swt[0][:, :], WT_h[:, :]), "sSP", 16)
                inc(EG.dma_start(swt[1][:, :], WT_l[:, :]), "sSP", 16)
                w(ES, "sSP")
                w(EV, "sSP")
        EY.wait_ge(sems["sT"], cnt["sT"])
        inc(EY.dma_start(AF[0:H, wLD[0]:wLD[0] + D], Wb_d[:, :]), "sLD", 16)
        w(ET, "sLD")
        use_bank(3)
        last = None
        for kc in range(8):
            last = ET.transpose(PBK[3][:, kc * 8:kc * 8 + 8],
                                AF[0:H, wLD[0] + kc * 128:wLD[0] + (kc + 1) * 128],
                                idn[0:H, 0:H])
        tc = inc(last, "sT")
        ES.wait_ge(sems["sT"], tc)
        ES.activation(WbT_h[:, :], PBK[3][:, 0:H * NKC], Act.Copy)
        sc = smark()
        EV.wait_ge(sems["sS"], sc)
        vc = inc(EV.tensor_tensor(WbT_l[:, :], PBK[3][:, 0:H * NKC], WbT_h[:, :],
                                  op=Alu.subtract), "sV")
        bank_guard[3] = ("sV", vc)

        mx_guard = [0]
        ck_guard = [0]
        fl_state = {"k_ok": 0}

        def front_lite(x_dram, row0):
            if mx_guard[0]:
                EY.wait_ge(sems["sV"], mx_guard[0])
            inc(EY.dma_start(af(mXT), x_dram[row0:row0 + 128, :]), "sLD3", 16)
            yield
            w(EV, "sLD3")
            m1 = inc(EV.tensor_reduce(lnm[:], af(mXT), axis=AX.X, op=Alu.add), "sQ")
            EV.wait_ge(sems["sQ"], m1)
            m2 = inc(EV.tensor_scalar(lnm[:], lnm[:], 1.0 / D, None, op0=Alu.mult), "sQ")
            EV.wait_ge(sems["sQ"], m2)
            m3 = inc(EV.tensor_scalar(af(mXN), af(mXT), lnm[:], None,
                                      op0=Alu.subtract), "sQ")
            yield
            EV.wait_ge(sems["sQ"], m3)
            m4 = inc(EV.tensor_tensor(af(mXT), af(mXN), af(mXN), op=Alu.mult), "sQ")
            EV.wait_ge(sems["sQ"], m4)
            m5 = inc(EV.tensor_reduce(lnv[:], af(mXT), axis=AX.X, op=Alu.add), "sQ")
            EV.wait_ge(sems["sQ"], m5)
            v2 = inc(EV.tensor_scalar(lnv[:], lnv[:], 1.0 / D, 1e-5,
                                      op0=Alu.mult, op1=Alu.add), "sV")
            yield
            w(ES, "sV", v2)
            s2 = inc(ES.activation(lnv[:], lnv[:], Act.Sqrt), "sS")
            w(EV, "sS", s2)
            m6 = inc(EV.reciprocal(lnv[:], lnv[:]), "sQ")
            EV.wait_ge(sems["sQ"], m6)
            m7 = inc(EV.tensor_scalar(af(mXN), af(mXN), lnv[:], None,
                                      op0=Alu.mult), "sV")
            yield
            w(EV, "sV", m7)
            m8 = inc(EV.tensor_tensor(af(mXT), af(mXN), af(mXN), op=Alu.mult), "sQ")
            EV.wait_ge(sems["sQ"], m8)
            v1 = inc(EV.tensor_reduce(rn[:], rh3(af(mXT)), axis=AX.X, op=Alu.add), "sV")
            mx_guard[0] = v1
            yield
            w(ES, "sV", v1)
            s3 = inc(ES.activation(rn[:], rn[:], Act.Sqrt), "sS")
            w(EV, "sS", s3)
            m9 = inc(EV.tensor_scalar(rn[:], rn[:], 1e-12, None, op0=Alu.max), "sQ")
            EV.wait_ge(sems["sQ"], m9)
            m10 = inc(EV.reciprocal(rn[:], rn[:]), "sQ")
            yield
            EV.wait_ge(sems["sQ"], m10)
            fl_state["k_ok"] = inc(EV.tensor_tensor(
                rh3(af(mKT)), rh3(af(mXN)),
                rn[:, :, None].broadcast_to((128, H, HD)), op=Alu.mult), "sV")

        def mixer_chunk(row0, with_o, o_dram, fl_next=None, spill=False, dmp=False):
            if spill and ck_guard[0]:
                w(EV, "sSP", ck_guard[0])
            k_ok = fl_state["k_ok"]
            w(ET, "sV")
            transpose_to(af(mXN), 8, 0, evac_split(AB[:, bXFH:bXFH + D],
                                                   AB[:, bXFL:bXFL + D], 0))
            if dmp:
                dump("xn", af(mXN))
                dump("kt", af(mKT))

            lh = [AB[:, bXFH + kc * 128:bXFH + (kc + 1) * 128] for kc in range(8)]
            ll = [AB[:, bXFL + kc * 128:bXFL + (kc + 1) * 128] for kc in range(8)]
            w(ET, "sV")
            vtc = []
            for pan in (0, 1):
                use_bank(1 + pan)
                rh = [WT_h[:, kc * D + pan * 512:kc * D + pan * 512 + 512] for kc in range(8)]
                rl = [WT_l[:, kc * D + pan * 512:kc * D + pan * 512 + 512] for kc in range(8)]
                vtc.append(mm3(1 + pan, 512, lh, ll, rh, rl))
            use_bank(7)
            rbh = [WbT_h[:, kc * H:(kc + 1) * H] for kc in range(8)]
            rbl = [WbT_l[:, kc * H:(kc + 1) * H] for kc in range(8)]
            btc = mm3(7, H, lh, ll, rbh, rbl)
            ES.wait_ge(sems["sT"], btc)
            bet_ok = inc(ES.activation(bet[:], PBK[7][:, 0:H], Act.Sigmoid), "sS")
            bank_guard[7] = ("sS", bet_ok)
            EV.wait_ge(sems["sS"], bet_ok)
            rrv = AF[:, mRR:mRR + 2048].rearrange("p (h t) -> p h t", h=H)
            for pan in (0, 1):
                EV.tensor_tensor(rrv[:, pan * 4:(pan + 1) * 4, 0:128],
                                 PBK[1 + pan][:, 0:512].rearrange("p (h d) -> p h d", h=4),
                                 bet[:, pan * 4:(pan + 1) * 4, None].broadcast_to((128, 4, HD)),
                                 op=Alu.mult)
            vkb = inc(EV.tensor_tensor(rh3(af(mKBT)), rh3(af(mKT)),
                                       bet[:, :, None].broadcast_to((128, H, HD)),
                                       op=Alu.mult), "sV")
            bank_guard[1] = ("sV", vkb)
            bank_guard[2] = ("sV", vkb)
            EV.tensor_tensor(rh3(af(mKDW)), rh3(af(mKT)),
                             dw_c[:, :, None].broadcast_to((128, H, HD)), op=Alu.mult)
            EV.wait_ge(sems["sV"], vkb)
            rr_ok = inc(EV.tensor_tensor(rrv[:, :, 128:256], rh3(af(mKBT)),
                                         de_c[:, :, None].broadcast_to((128, H, HD)),
                                         op=Alu.mult), "sV")
            if dmp:
                dump("bet", bet[:])
                dump("R0", AF[:, mRR:mRR + 2048])
                dump("kdw", af(mKDW))

            w(ET, "sV", rr_ok)
            transpose_to(af(mKT), 8, 0, evac_copy(af(mKFM), 0))
            transpose_to(af(mKBT), 8, 0, evac_copy(af(mKBFM), 0))
            w(EV, "sV")
            kdef_ok = inc(EV.tensor_tensor(af(mKDEF), af(mKFM), de_f[:, :],
                                           op=Alu.mult), "sV")

            kf = [af(mKFM)[:, h * 128:(h + 1) * 128] for h in range(H)]
            kbf = [af(mKBFM)[:, h * 128:(h + 1) * 128] for h in range(H)]
            w(ET, "sV", kdef_ok)
            for grp in (0, 1):
                use_bank(3 + grp)
            last = None
            for h in range(H):
                last = ET.matmul(PBK[3 + h // 4][:, (h % 4) * 128:(h % 4) * 128 + 128],
                                 kbf[h], kf[h], start=True, stop=True)
            tc = inc(last, "sT")
            EV.wait_ge(sems["sT"], tc)
            vc = 0
            for grp in (0, 1):
                vc = inc(EV.tensor_tensor(AF[:, mPP + grp * 512:mPP + (grp + 1) * 512],
                                          PBK[3 + grp][:, 0:512],
                                          nLs[:, grp * 512:(grp + 1) * 512],
                                          op=Alu.mult), "sV")
            bank_guard[3] = ("sV", vc)
            bank_guard[4] = ("sV", vc)
            if dmp:
                dump("A0", AF[:, mPP:mPP + 1024])
                dump("kdef", af(mKDEF))
            if True:
                w(ET, "sV", vc)
                for grp in (0, 1):
                    use_bank(3 + grp)
                last = None
                for h in range(H):
                    last = ET.matmul(PBK[3 + h // 4][:, (h % 4) * 128:(h % 4) * 128 + 128],
                                     kf[h], kf[h], start=True, stop=True)
                tc = inc(last, "sT")
                EV.wait_ge(sems["sT"], tc)
                vc = 0
                for grp in (0, 1):
                    vc = inc(EV.tensor_tensor(
                        AF[:, mATT + grp * 512:mATT + (grp + 1) * 512],
                        PBK[3 + grp][:, 0:512],
                        Lm[:, grp * 512:(grp + 1) * 512], op=Alu.mult), "sV")
                bank_guard[3] = ("sV", vc)
                bank_guard[4] = ("sV", vc)

            w(ET, "sV")
            for (j0, nb) in ((0, 4), (4, 4)):
                use_bank(0)
                last = None
                for j in range(nb):
                    h = j0 + j
                    last = ET.transpose(PBK[0][:, j * 128:(j + 1) * 128],
                                        AF[:, mPP + h * 128:mPP + (h + 1) * 128], idn[:])
                tc = inc(last, "sT")
                ES.wait_ge(sems["sT"], tc)
                ii = ES.activation(AF[:, mPPT + j0 * 128:mPPT + (j0 + nb) * 128],
                                   PBK[0][:, 0:nb * 128], Act.Copy)
                sc = inc(ii, "sS")
                bank_guard[0] = ("sS", sc)
            pp_ok = sc
            rr_mark = 0
            ppt = [af(mPPT)[:, h * 128:(h + 1) * 128] for h in range(H)]
            ppn = [af(mPP)[:, h * 128:(h + 1) * 128] for h in range(H)]
            for lev in range(7):
                if fl_next is not None:
                    next(fl_next, None)
                ET.wait_ge(sems["sS"], pp_ok)
                for b in (1, 2, 5, 6):
                    use_bank(b)
                last = None
                for h in range(H):
                    bnk = [1, 2, 5, 6][h // 2]
                    col = (h % 2) * 256
                    last = ET.matmul(PBK[bnk][:, col:col + 256], ppt[h],
                                     AF[:, mRR + h * 256:mRR + (h + 1) * 256],
                                     start=True, stop=True)
                tc_ap = inc(last, "sT")
                if lev < 6:
                    for b in (3, 4, 0, 7):
                        use_bank(b)
                    last = None
                    for h in range(H):
                        last = ET.matmul(PBK[3 + h // 4][:, (h % 4) * 128:(h % 4) * 128 + 128],
                                         ppt[h], ppn[h], start=True, stop=True)
                    for h in range(H):
                        last = ET.matmul(PBK[(0, 7)[h // 4]][:, (h % 4) * 128:(h % 4) * 128 + 128],
                                         ppn[h], ppt[h], start=True, stop=True)
                    tc_sq = inc(last, "sT")
                    ES.wait_ge(sems["sT"], tc_sq)
                    ES.activation(AF[:, mPP:mPP + 512], PBK[3][:, 0:512], Act.Copy)
                    ES.activation(AF[:, mPP + 512:mPP + 1024], PBK[4][:, 0:512], Act.Copy)
                    ES.activation(AF[:, mPPT:mPPT + 512], PBK[0][:, 0:512], Act.Copy)
                    ii = ES.activation(AF[:, mPPT + 512:mPPT + 1024], PBK[7][:, 0:512],
                                       Act.Copy)
                    pp_ok = inc(ii, "sS")
                    for b in (3, 4, 0, 7):
                        bank_guard[b] = ("sS", pp_ok)
                EV.wait_ge(sems["sT"], tc_ap)
                if rr_mark:
                    EV.wait_ge(sems["sV"], rr_mark)
                last = None
                for bq in range(4):
                    bnk = [1, 2, 5, 6][bq]
                    last = EV.tensor_tensor(AF[:, mRR + bq * 512:mRR + (bq + 1) * 512],
                                            AF[:, mRR + bq * 512:mRR + (bq + 1) * 512],
                                            PBK[bnk][:, 0:512], op=Alu.add)
                vc = inc(last, "sV")
                rr_mark = vc
                for bq in range(4):
                    bank_guard[[1, 2, 5, 6][bq]] = ("sV", vc)
            w(EV, "sV", rr_mark)

            w(ET, "sV")
            for grp in (0, 1):
                use_bank(3 + grp)
            last = None
            for h in range(H):
                last = ET.transpose(PBK[3 + h // 4][:, (h % 4) * 128:(h % 4) * 128 + 128],
                                    AF[:, mRR + h * 256 + 128:mRR + h * 256 + 256], idn[:])
            tc = inc(last, "sT")
            EV.wait_ge(sems["sT"], tc)
            vc = 0
            for grp in (0, 1):
                vc = inc(EV.tensor_copy(AF[:, mKCDT + grp * 512:mKCDT + (grp + 1) * 512],
                                        PBK[3 + grp][:, 0:512]), "sV")
            bank_guard[3] = ("sV", vc)
            bank_guard[4] = ("sV", vc)
            w(ET, "sV", vc)
            for grp in (0, 1):
                use_bank(3 + grp)
            last = None
            for h in range(H):
                last = ET.matmul(PBK[3 + h // 4][:, (h % 4) * 128:(h % 4) * 128 + 128],
                                 AF[:, mKCDT + h * 128:mKCDT + (h + 1) * 128],
                                 Smat[:, h * 128:(h + 1) * 128], start=True, stop=True)
            tc = inc(last, "sT")
            EV.wait_ge(sems["sT"], tc)
            rrv = AF[:, mRR:mRR + 2048].rearrange("p (h t) -> p h t", h=H)
            for grp in (0, 1):
                vnew_ok = inc(EV.scalar_tensor_tensor(
                    af(mVNEW)[:, grp * 512:(grp + 1) * 512].rearrange("p (h d) -> p h d", h=4),
                    PBK[3 + grp][:, 0:512].rearrange("p (h d) -> p h d", h=4), -1.0,
                    rrv[:, grp * 4:(grp + 1) * 4, 0:128],
                    op0=Alu.mult, op1=Alu.add), "sV")
            bank_guard[3] = ("sV", vnew_ok)
            bank_guard[4] = ("sV", vnew_ok)

            if dmp:
                dump("vnew", af(mVNEW))
                if with_o:
                    dump("att", AF[:, mATT:mATT + 1024])
            w(ET, "sV", vnew_ok)
            for grp in (0, 1):
                use_bank(5 + grp)
            last = None
            for h in range(H):
                last = ET.transpose(PBK[5 + h // 4][:, (h % 4) * 128:(h % 4) * 128 + 128],
                                    AF[:, mATT + h * 128:mATT + (h + 1) * 128], idn[:])
            tc = inc(last, "sT")
            EV.wait_ge(sems["sT"], tc)
            attT_ok = 0
            for grp in (0, 1):
                attT_ok = inc(EV.tensor_copy(
                    AF[:, mATTT + grp * 512:mATTT + (grp + 1) * 512],
                    PBK[5 + grp][:, 0:512]), "sV")
            bank_guard[5] = ("sV", attT_ok)
            bank_guard[6] = ("sV", attT_ok)
            if with_o:
                w(ET, "sV", attT_ok)
                for grp in (0, 1):
                    use_bank(5 + grp)
                last = None
                for h in range(H):
                    bnk = 5 + h // 4
                    col = (h % 4) * 128
                    ET.matmul(PBK[bnk][:, col:col + 128],
                              af(mKDEF)[:, h * 128:(h + 1) * 128],
                              Smat[:, h * 128:(h + 1) * 128], start=True, stop=False)
                    last = ET.matmul(PBK[bnk][:, col:col + 128],
                                     AF[:, mATTT + h * 128:mATTT + (h + 1) * 128],
                                     af(mVNEW)[:, h * 128:(h + 1) * 128],
                                     start=False, stop=True)
                tc = inc(last, "sT")
                EV.wait_ge(sems["sT"], tc)
                w(EV, "sSP")
                vc = 0
                for grp in (0, 1):
                    vc = inc(EV.tensor_copy(af(mOT)[:, grp * 512:(grp + 1) * 512],
                                            PBK[5 + grp][:, 0:512]), "sV")
                bank_guard[5] = ("sV", vc)
                bank_guard[6] = ("sV", vc)
                w(EG, "sV", vc)
                inc(EG.dma_start(o_dram[row0:row0 + 128, :], af(mOT)), "sSP", 16)

            w(ET, "sV", vnew_ok)
            for grp in (0, 1):
                use_bank(3 + grp)
            last = None
            for h in range(H):
                last = ET.matmul(PBK[3 + h // 4][:, (h % 4) * 128:(h % 4) * 128 + 128],
                                 af(mKDW)[:, h * 128:(h + 1) * 128],
                                 af(mVNEW)[:, h * 128:(h + 1) * 128], start=True, stop=True)
            tc = inc(last, "sT")
            EV.wait_ge(sems["sT"], tc)
            ms = inc(EV.tensor_tensor(rh3(Smat[:]), rh3(Smat[:]),
                                      ctd_c[:, :, None].broadcast_to((128, H, HD)),
                                      op=Alu.mult), "sQ")
            EV.wait_ge(sems["sQ"], ms)
            vc = 0
            for grp in (0, 1):
                vc = inc(EV.tensor_tensor(Smat[:, grp * 512:(grp + 1) * 512],
                                          Smat[:, grp * 512:(grp + 1) * 512],
                                          PBK[3 + grp][:, 0:512], op=Alu.add), "sV")
            bank_guard[3] = ("sV", vc)
            bank_guard[4] = ("sV", vc)
            if spill:
                w(EG, "sV", attT_ok)
                inc(EG.dma_start(sck_d[row0:row0 + 128, :], af(mSPILL, mSPILLW)),
                    "sSP", 16)
                ck_guard[0] = cnt["sSP"]

        fl = front_lite(xo_d, 0)
        for _ in fl:
            pass
        for n in range(NCH):
            pump_wq(2)
            fl_next = front_lite(xo_d, (n + 1) * C) if n + 1 < NCH else None
            mixer_chunk(n * C, False, None, fl_next=fl_next, spill=True,
                        dmp=(DBG and n == 0))
            if fl_next is not None:
                for _ in fl_next:
                    pass

        BK4 = [1, 2, 7, 0]
        spill_guard = {}

        def xqv(gp, piece, kc, tt, n=128):
            base = (8192 if piece == 0 else 12288) + kc * 256 + tt * 64
            return AF[:, base:base + n // 2].bitcast(BF16)

        xq_ok = [0, 0]
        xq_free = [0, 0]

        def prep_group(g4):
            gp = (g4 // 4) % 2
            ntile = min(4, NT - g4)
            for tt in range(ntile):
                tile = g4 + tt
                w(EY, "sV")
                inc(EY.dma_start(af(fXT), xo_d[tile * 128:(tile + 1) * 128, :]), "sLD3", 16)
                w(EV, "sLD3")
                lv = layernorm(fXT, fXN)
                w(ES, "sV")
                w(EG, "sV")
                w(EV, "sV", lv)
                quant(af(fXN), D, split=False)
                w(ET, "sV")
                w(ET, "sG")
                if xq_free[gp]:
                    ET.wait_ge(sems[rs("sT")], xq_free[gp])

                def evx(tt_):
                    def ev(j0, nb, tc):
                        ES.wait_ge(sems[rs("sT")], tc)
                        sc = 0
                        for j in range(nb):
                            kc = j0 + j
                            sc = inc(ES.activation(xqv(gp, 0, kc, tt_),
                                                   PBK[0][:, j * 128:(j + 1) * 128],
                                                   Act.Copy), "sS")
                        EV.wait_ge(sems[rs("sS")], sc)
                        vc = 0
                        for j in range(nb):
                            kc = j0 + j
                            vc = inc(EV.tensor_tensor(xqv(gp, 1, kc, tt_),
                                                      PBK[0][:, j * 128:(j + 1) * 128],
                                                      xqv(gp, 0, kc, tt_),
                                                      op=Alu.subtract), "sV")
                        bank_guard[0] = (rs("sV"), vc)
                    return ev
                transpose_to(af(qXQ), 8, 0, evx(tt))
                yield
            xq_ok[gp] = cnt[rs("sV")]

        def h_chain(g4, tt, pw, po, up_tc_tt):
            tile = g4 + tt
            EV.wait_ge(sems[rs("sT")], up_tc_tt)
            vc = inc(EV.tensor_tensor(AF[:, fHT + tt * 512:fHT + tt * 512 + pw],
                                      AF[:, fHT + tt * 512:fHT + tt * 512 + pw],
                                      PBK[BK4[tt]][:, 0:pw], op=Alu.mult), "sV")
            bank_guard[BK4[tt]] = (rs("sV"), vc)
            w(EV, "sV", vc)
            par = tt % 2
            psem = "sP0" if par == 0 else "sP1"
            qh_o, ql_o = bQH + par * 512, bQL + par * 512
            if par in spill_guard:
                w(ES, psem, spill_guard[par])
                w(EG, psem, spill_guard[par])
            w(ES, "sV", vc)
            w(EG, "sV", vc)
            quant(AF[:, fHT + tt * 512:fHT + tt * 512 + pw], pw, split=True,
                  qh_off=qh_o, ql_off=ql_o)
            w(EG, "sG")
            inc(EG.dma_start(shq[0][tile * 128:(tile + 1) * 128, po:po + pw],
                             ab(qh_o, pw)), psem, 16)
            inc(EG.dma_start(shq[1][tile * 128:(tile + 1) * 128, po:po + pw],
                             ab(ql_o, pw)), psem, 16)
            spill_guard[par] = cnt[psem]

        def p3a_gen():
            yield from prep_group(0)
            wave_guard = [0, 0]
            for g4 in range(0, NT, 4):
                gp = (g4 // 4) % 2
                ntile = min(4, NT - g4)
                for pan in range(6):
                    pw, po = PANW[pan], PANO[pan]
                    for wi, scrw in [(0, swg), (1, swu)]:
                        wv = (pan * 2 + wi) % 2
                        wsem = "sLD0" if wv == 0 else "sLD1"
                        base = bWP[wv]
                        EY.wait_ge(sems[rs("sT")], wave_guard[wv])
                        for kc in range(8):
                            inc(EY.dma_start_transpose(
                                AB[:, base + kc * 512:base + kc * 512 + pw],
                                scrw[0][po:po + pw, kc * 128:(kc + 1) * 128]), wsem, 16)
                            inc(EY.dma_start_transpose(
                                AB[:, base + 4096 + kc * 512:base + 4096 + kc * 512 + pw],
                                scrw[1][po:po + pw, kc * 128:(kc + 1) * 128]), wsem, 16)
                        w(ET, wsem)
                        if wi == 0:
                            w(ET, "sV", xq_ok[gp])
                        rh = [AB[:, base + kc * 512:base + kc * 512 + pw] for kc in range(8)]
                        rl = [AB[:, base + 4096 + kc * 512:base + 4096 + kc * 512 + pw]
                              for kc in range(8)]
                        up_tc = []
                        for tt in range(ntile):
                            lh = [xqv(gp, 0, kc, tt) for kc in range(8)]
                            ll = [xqv(gp, 1, kc, tt) for kc in range(8)]
                            use_bank(BK4[tt])
                            tc = mm3(BK4[tt], pw, lh, ll, rh, rl)
                            if wi == 0:
                                ES.wait_ge(sems[rs("sT")], tc)
                                par = tt % 2
                                if par in spill_guard:
                                    w(ES, "sP0" if par == 0 else "sP1",
                                      spill_guard[par])
                                sc = inc(ES.activation(
                                    AF[:, fHT + tt * 512:fHT + tt * 512 + pw],
                                    PBK[BK4[tt]][:, 0:pw], Act.Silu), "sS")
                                bank_guard[BK4[tt]] = (rs("sS"), sc)
                            else:
                                up_tc.append(tc)
                                if tt >= 2:
                                    h_chain(g4, tt - 2, pw, po, up_tc[tt - 2])
                        wave_guard[wv] = cnt[rs("sT")]
                        if wi == 1:
                            for tt in range(max(0, ntile - 2), ntile):
                                h_chain(g4, tt, pw, po, up_tc[tt])
                        yield
                if pan == 5 and g4 + 4 < NT:
                    xq_free[0] = cnt[rs("sT")]
                    yield from prep_group(g4 + 4)

        def p3a_redirected():
            gen = p3a_gen()
            while True:
                R["on"] = True
                try:
                    next(gen)
                except StopIteration:
                    R["on"] = False
                    return
                R["on"] = False
                yield

        p3a = p3a_redirected()

        pump_wq(len(wq_slices))
        spill_guard[0] = cnt["sP0"]
        spill_guard[1] = cnt["sP1"]
        EY.wait_ge(sems["sP0"], cnt["sP0"])
        EY.wait_ge(sems["sP1"], cnt["sP1"])
        EY.wait_ge(sems["sSP"], cnt["sSP"])
        w(EV, "sIN")
        xc_ok = inc(EV.tensor_tensor(af(mPP, H * 128), Smat[:],
                                     ismb[:].broadcast_to((128, H * 128)),
                                     op=Alu.mult), "sV")
        w(EG, "sV", xc_ok)
        inc(EG.dma_start(ccin_t.ap(), af(mPP, H * 128)), "sCC", 16)
        w(EG, "sCC")
        cc = inc(EG.collective_compute(
            "AllReduce", Alu.add,
            replica_groups=[[i, i + 1] for i in range(0, n_cores, 2)],
            ins=[ccin_t.ap().opt()],
            outs=[ccout_t.ap().opt()]), "sG")
        EG.wait_ge(sems["sG"], cc)
        inc(EG.dma_start(af(mPP, H * 128), ccout_t.ap()), "sCC", 16)
        w(EG, "sCC")
        g_ok = inc(EG.nop(), "sG")
        w(EV, "sG", g_ok)
        inc(EV.tensor_tensor(Smat[:], af(mPP, H * 128),
                             smb[:].broadcast_to((128, H * 128)), op=Alu.mult), "sV")

        for n in range(NCH):
            next(p3a, None)
            next(p3a, None)
            next(p3a, None)
            row0 = n * C
            EY.wait_ge(sems["sT"], cnt["sT"])
            EY.wait_ge(sems["sV"], cnt["sV"])
            inc(EY.dma_start(af(mSPILL, mSPILLW), sck_d[row0:row0 + 128, :]), "sLD", 16)
            w(ET, "sLD")
            for grp in (0, 1):
                use_bank(3 + grp)
            last = None
            for h in range(H):
                last = ET.transpose(PBK[3 + h // 4][:, (h % 4) * 128:(h % 4) * 128 + 128],
                                    AF[:, mRR + h * 256 + 128:mRR + h * 256 + 256], idn[:])
            tc = inc(last, "sT")
            EV.wait_ge(sems["sT"], tc)
            vc = 0
            for grp in (0, 1):
                vc = inc(EV.tensor_copy(AF[:, mKCDT + grp * 512:mKCDT + (grp + 1) * 512],
                                        PBK[3 + grp][:, 0:512]), "sV")
            bank_guard[3] = ("sV", vc)
            bank_guard[4] = ("sV", vc)
            w(ET, "sV", vc)
            for grp in (0, 1):
                use_bank(3 + grp)
            last = None
            for h in range(H):
                last = ET.matmul(PBK[3 + h // 4][:, (h % 4) * 128:(h % 4) * 128 + 128],
                                 AF[:, mKCDT + h * 128:mKCDT + (h + 1) * 128],
                                 Smat[:, h * 128:(h + 1) * 128], start=True, stop=True)
            tc = inc(last, "sT")
            EV.wait_ge(sems["sT"], tc)
            rrv = AF[:, mRR:mRR + 2048].rearrange("p (h t) -> p h t", h=H)
            for grp in (0, 1):
                vnew_ok = inc(EV.scalar_tensor_tensor(
                    af(mVNEW)[:, grp * 512:(grp + 1) * 512].rearrange("p (h d) -> p h d", h=4),
                    PBK[3 + grp][:, 0:512].rearrange("p (h d) -> p h d", h=4), -1.0,
                    rrv[:, grp * 4:(grp + 1) * 4, 0:128],
                    op0=Alu.mult, op1=Alu.add), "sV")
            bank_guard[3] = ("sV", vnew_ok)
            bank_guard[4] = ("sV", vnew_ok)
            w(ET, "sV", vnew_ok)
            for grp in (0, 1):
                use_bank(5 + grp)
            last = None
            for h in range(H):
                bnk = 5 + h // 4
                col = (h % 4) * 128
                ET.matmul(PBK[bnk][:, col:col + 128],
                          af(mKDEF)[:, h * 128:(h + 1) * 128],
                          Smat[:, h * 128:(h + 1) * 128], start=True, stop=False)
                last = ET.matmul(PBK[bnk][:, col:col + 128],
                                 AF[:, mATTT + h * 128:mATTT + (h + 1) * 128],
                                 af(mVNEW)[:, h * 128:(h + 1) * 128],
                                 start=False, stop=True)
            tc = inc(last, "sT")
            EV.wait_ge(sems["sT"], tc)
            w(EV, "sSP")
            vc = 0
            for grp in (0, 1):
                vc = inc(EV.tensor_copy(af(mOT)[:, grp * 512:(grp + 1) * 512],
                                        PBK[5 + grp][:, 0:512]), "sV")
            bank_guard[5] = ("sV", vc)
            bank_guard[6] = ("sV", vc)
            w(EG, "sV", vc)
            inc(EG.dma_start(so_d[row0:row0 + 128, :], af(mOT)), "sSP", 16)
            w(ET, "sV", vnew_ok)
            for grp in (0, 1):
                use_bank(3 + grp)
            last = None
            for h in range(H):
                last = ET.matmul(PBK[3 + h // 4][:, (h % 4) * 128:(h % 4) * 128 + 128],
                                 af(mKDW)[:, h * 128:(h + 1) * 128],
                                 af(mVNEW)[:, h * 128:(h + 1) * 128], start=True, stop=True)
            tc = inc(last, "sT")
            EV.wait_ge(sems["sT"], tc)
            ms = inc(EV.tensor_tensor(rh3(Smat[:]), rh3(Smat[:]),
                                      ctd_c[:, :, None].broadcast_to((128, H, HD)),
                                      op=Alu.mult), "sQ")
            EV.wait_ge(sems["sQ"], ms)
            vc = 0
            for grp in (0, 1):
                vc = inc(EV.tensor_tensor(Smat[:, grp * 512:(grp + 1) * 512],
                                          Smat[:, grp * 512:(grp + 1) * 512],
                                          PBK[3 + grp][:, 0:512], op=Alu.add), "sV")
            bank_guard[3] = ("sV", vc)
            bank_guard[4] = ("sV", vc)

        for _ in p3a:
            pass
        def afb(c0, n):
            return AF[:, c0:c0 + n].bitcast(BF16)

        EY.wait_ge(sems["sT"], cnt["sT"])
        EY.wait_ge(sems["sSP"], cnt["sSP"])
        EY.wait_ge(sems["sP0"], cnt["sP0"])
        EY.wait_ge(sems["sP1"], cnt["sP1"])
        inc(EY.dma_start(WT_h[:, :], swt[0][:, :]), "sLD", 16)
        inc(EY.dma_start(WT_l[:, :], swt[1][:, :]), "sLD", 16)
        stage_free = [0, 0]
        fm_ready = [0, 0]
        g_guard = [0, 0]
        ot_free = [0, 0]
        ld3_mark = [0, 0]
        xt_guard = [0]
        for hp in range(2):
            EY.wait_ge(sems["sT"], cnt["sT"])
            EY.wait_ge(sems["sSP"], cnt["sSP"])
            EY.wait_ge(sems["sP0"], cnt["sP0"])
            EY.wait_ge(sems["sP1"], cnt["sP1"])
            g_guard = [0, 0]
            for ic in range(NIC):
                inc(EY.dma_start_transpose(
                    AB[:, bWD_H + ic * 512:bWD_H + ic * 512 + 512],
                    swd[0][hp * 512:(hp + 1) * 512, ic * 128:(ic + 1) * 128]), "sLD", 16)
                inc(EY.dma_start_transpose(
                    AB[:, bWD_L + ic * 512:bWD_L + ic * 512 + 512],
                    swd[1][hp * 512:(hp + 1) * 512, ic * 128:(ic + 1) * 128]), "sLD", 16)
            for tile in range(NT):
                par = tile % 2
                psem2p = "sP0" if par == 0 else "sP1"
                if hp == 1:
                    pGHp = fGH if par == 0 else fGH2
                    pOTp = fOT if par == 0 else fOT2
                    if g_guard[par]:
                        EY.wait_ge(sems[psem2p], g_guard[par])
                    if ot_free[par]:
                        EY.wait_ge(sems["sT"], ot_free[par])
                    if xt_guard[0]:
                        EY.wait_ge(sems["sV"], xt_guard[0])
                    inc(EY.dma_start(AF[:, pGHp + 512:pGHp + 1024],
                                     sg_d[tile * 128:(tile + 1) * 128, :]), "sLD3", 16)
                    inc(EY.dma_start(af(pOTp), so_d[tile * 128:(tile + 1) * 128, :]),
                        "sLD3", 16)
                    inc(EY.dma_start(af(fXT), xo_d[tile * 128:(tile + 1) * 128, :]),
                        "sLD3", 16)
                    ld3_mark[par] = cnt["sLD3"]
                EY.wait_ge(sems["sT"], stage_free[par])
                hsem = "sLD0" if par == 0 else "sLD1"
                inc(EY.dma_start(afb(fSTG[par][0], 1408), shq[0][tile * 128:(tile + 1) * 128, :]),
                    hsem, 16)
                inc(EY.dma_start(afb(fSTG[par][1], 1408), shq[1][tile * 128:(tile + 1) * 128, :]),
                    hsem, 16)
                hq_lc = cnt[hsem]
                ET.wait_ge(sems[hsem], hq_lc)
                src_list = [(fSTG[par][pc], fFM[par][pc]) for pc in range(2)]
                blocks = [(pc, ic) for pc in range(2) for ic in range(NIC)]
                bi = 0
                while bi < len(blocks):
                    nb = min(8, len(blocks) - bi)
                    use_bank(0)
                    last = None
                    for j in range(nb):
                        pc, ic = blocks[bi + j]
                        last = ET.transpose(
                            PBK[0][:, j * 64:j * 64 + 64].bitcast(BF16),
                            afb(fSTG[par][pc], 1408)[:, ic * 128:(ic + 1) * 128],
                            idnb[:])
                    tc = inc(last, "sT")
                    EV.wait_ge(sems["sT"], tc)
                    vc = 0
                    for j in range(nb):
                        pc, ic = blocks[bi + j]
                        vc = inc(EV.tensor_copy(
                            afb(fFM[par][pc], 1408)[:, ic * 128:(ic + 1) * 128],
                            PBK[0][:, j * 64:j * 64 + 64].bitcast(BF16)), "sV")
                    bank_guard[0] = ("sV", vc)
                    bi += nb
                stage_free[par] = cnt["sT"]
                fm_ready[par] = cnt["sV"]
                w(ET, "sLD")
                w(ET, "sV", fm_ready[par])
                psb = 1 + par
                use_bank(psb)
                lh = [afb(fFM[par][0], 1408)[:, ic * 128:(ic + 1) * 128] for ic in range(NIC)]
                ll = [afb(fFM[par][1], 1408)[:, ic * 128:(ic + 1) * 128] for ic in range(NIC)]
                rh = [AB[:, bWD_H + ic * 512:bWD_H + (ic + 1) * 512] for ic in range(NIC)]
                rl = [AB[:, bWD_L + ic * 512:bWD_L + (ic + 1) * 512] for ic in range(NIC)]
                tc = mm3(psb, 512, lh, ll, rh, rl)
                ES.wait_ge(sems["sT"], tc)
                pXN = fXN if par == 0 else fXN2
                pST = fST if par == 0 else fST2
                pGH = fGH if par == 0 else fGH2
                pOT = fOT if par == 0 else fOT2
                pOF = bOFH if par == 0 else bOF2
                psem2 = "sP0" if par == 0 else "sP1"
                if hp == 0:
                    if g_guard[par]:
                        w(ES, psem2, g_guard[par])
                    sc = inc(ES.activation(af(pGH, 512), PBK[psb][:, 0:512],
                                           Act.Silu), "sS")
                    bank_guard[psb] = ("sS", sc)
                    w(EG, "sS", sc)
                    inc(EG.dma_start(sg_d[tile * 128:(tile + 1) * 128, :], af(pGH, 512)),
                        psem2, 16)
                    g_guard[par] = cnt[psem2]
                else:
                    if g_guard[par]:
                        w(ES, psem2, g_guard[par])
                    sc = inc(ES.activation(af(pGH, 512), PBK[psb][:, 0:512],
                                           Act.Silu), "sS")
                    bank_guard[psb] = ("sS", sc)
                    w(EV, "sLD3", ld3_mark[par])
                    ln_ok = layernorm(fXT, pXN, sq_off=fSQ3)
                    ot_free_pending = True
                    xt_guard[0] = cnt["sV"]
                    w(ET, "sV")
                    transpose_to(af(pOT), 8, 0, evac_split(AB[:, pOF:pOF + D],
                                                           AB[:, pOF + 1024:pOF + 1024 + D]
                                                           if False else
                                                           AB[:, pOF + 1024:pOF + 2048], 0))
                    of_ok = cnt["sV"]
                    ot_free[par] = cnt["sT"]
                    lh2 = [AB[:, pOF + kc * 128:pOF + (kc + 1) * 128] for kc in range(8)]
                    ll2 = [AB[:, pOF + 1024 + kc * 128:pOF + 1024 + (kc + 1) * 128]
                           for kc in range(8)]
                    w(ET, "sV", of_ok)
                    w(ET, "sLD")
                    rtc = []
                    for pan2 in (0, 1):
                        use_bank(5 + pan2)
                        rh2 = [WT_h[:, kc * D + pan2 * 512:kc * D + pan2 * 512 + 512]
                               for kc in range(8)]
                        rl2 = [WT_l[:, kc * D + pan2 * 512:kc * D + pan2 * 512 + 512]
                               for kc in range(8)]
                        rtc.append(mm3(5 + pan2, 512, lh2, ll2, rh2, rl2))
                    EV.wait_ge(sems["sT"], rtc[-1])
                    EV.wait_ge(sems["sV"], ln_ok)
                    vc = 0
                    for pan2 in (0, 1):
                        vc = inc(EV.tensor_tensor(
                            AF[:, pST + pan2 * 512:pST + (pan2 + 1) * 512],
                            af(pXN)[:, pan2 * 512:(pan2 + 1) * 512],
                            PBK[5 + pan2][:, 0:512], op=Alu.add), "sV")
                    bank_guard[5] = ("sV", vc)
                    bank_guard[6] = ("sV", vc)
                    w(EV, "sS", sc)
                    EV.wait_ge(sems["sV"], vc)
                    mg = 0
                    mg = inc(EV.tensor_tensor(AF[:, pST:pST + 512], AF[:, pST:pST + 512],
                                              AF[:, pGH + 512:pGH + 1024],
                                              op=Alu.mult), "sQ")
                    mg = inc(EV.tensor_tensor(AF[:, pST + 512:pST + 1024],
                                              AF[:, pST + 512:pST + 1024],
                                              AF[:, pGH:pGH + 512], op=Alu.mult), "sQ")
                    EV.wait_ge(sems["sQ"], mg)
                    last = EV.tensor_tensor(AF[:, pST:pST + 1024], AF[:, pST:pST + 1024],
                                            af(fXT), op=Alu.add)
                    vc = inc(last, "sV")
                    xt_guard[0] = vc
                    w(EG, "sV", vc)
                    inc(EG.dma_start(y_d[tile * 128:(tile + 1) * 128, :],
                                     AF[:, pST:pST + 1024]), psem2, 16)
                    g_guard[par] = cnt[psem2]

        EG.wait_ge(sems["sSP"], cnt["sSP"])
        EG.wait_ge(sems["sP0"], cnt["sP0"])
        EG.wait_ge(sems["sP1"], cnt["sP1"])
    return nc


def kernel(**inputs):
    xs = np.ascontiguousarray(inputs["L0_attn_out"], dtype=np.float32)
    B, T, _ = xs.shape
    t_own = T // 2
    in_maps = []
    for core in range(8):
        b, half = core // 2, core % 2
        own = xs[b, half * t_own:(half + 1) * t_own]
        in_maps.append({
            "xo": np.ascontiguousarray(own),
            "smask": np.full((1, 1), 1.0 if half == 1 else 0.0, np.float32),
            "Ww": np.ascontiguousarray(inputs["L0_attn_W_write"], dtype=np.float32),
            "Wr": np.ascontiguousarray(inputs["L0_attn_W_read"], dtype=np.float32),
            "Wb": np.ascontiguousarray(inputs["L0_attn_W_beta"], dtype=np.float32),
            "dec": np.ascontiguousarray(inputs["L0_attn_decay"], dtype=np.float32).reshape(1, H),
            "wg": np.ascontiguousarray(inputs["L0_ffn_w_gate"], dtype=np.float32),
            "wu": np.ascontiguousarray(inputs["L0_ffn_w_up"], dtype=np.float32),
            "wd": np.ascontiguousarray(inputs["L0_ffn_w_down"], dtype=np.float32),
        })
    nc = build_nc(t_own=t_own, n_cores=8)
    res = run_bass_kernel_spmd(nc, in_maps, core_ids=list(range(8)))
    out = np.zeros((B, T, D), np.float32)
    for core in range(8):
        b, half = core // 2, core % 2
        out[b, half * t_own:(half + 1) * t_own] = res.results[core]["y"]
    return out
